# version 28
# baseline (speedup 1.0000x reference)
import numpy as np
from contextlib import ExitStack

import concourse.bass as bass
import concourse.mybir as mybir
from concourse.bass import IndirectOffsetOnAxis
from concourse.bass_utils import run_bass_kernel_spmd

F32 = mybir.dt.float32
BF16 = mybir.dt.bfloat16
U32 = mybir.dt.uint32
I32 = mybir.dt.int32
ALU = mybir.AluOpType
AF = mybir.ActivationFunctionType
AX = mybir.AxisListType.X

S, D, H, DH, E, FF = 4096, 1024, 16, 64, 8, 4096
NT = S // 128
NCORES = 8
BISECT = 28
PAD = 128
SP = S + PAD


def host_consts():
    c = {}
    c["ident"] = np.eye(128, dtype=np.float32)
    kk = np.arange(128)[:, None]
    qq = np.arange(512)[None, :]
    for b in range(4):
        c[f"bandmask{b}"] = np.where(kk + b * 128 > qq, -1e9, 0.0).astype(np.float32)
    k2 = np.arange(128)
    c["blocktri"] = ((k2[:, None] // 16 == k2[None, :] // 16)
                     & (k2[:, None] < k2[None, :])).astype(np.float32)
    c["e16"] = (np.arange(128)[None, :] // 16 == np.arange(8)[:, None]).astype(np.float32)
    c["e16t"] = c["e16"].T.copy()
    c["ones8"] = np.ones((8, 1), np.float32)
    c["ones1x8"] = np.ones((1, 8), np.float32)
    c["ones1x64"] = np.ones((1, 64), np.float32)
    c["ones1x128"] = np.ones((1, 128), np.float32)
    p = np.arange(128)[:, None]
    f = np.arange(256)[None, :]
    tokid = ((p % 16) * 256 + f).astype(np.float32)
    c["tokid_u32"] = tokid.astype(np.uint32)
    c["tok8k"] = (tokid + 8192.0).astype(np.float32)
    c["basem1"] = ((p // 16) * 1024 - 1).astype(np.float32)
    c["bound"] = ((p // 16) * 1024 + 1023.5).astype(np.float32)
    c["minus20"] = np.full((128, 1), -20.0, np.float32)
    c["eps"] = np.full((128, 1), 1e-5, np.float32)
    padinit = np.zeros((128, 16), np.uint32)
    padinit[:, 0::2] = (4096 + np.arange(128))[:, None].astype(np.uint32)
    padinit[:, 1::2] = np.float32(0.0).view(np.uint32)
    c["padinit"] = padinit
    inv_freq = (1.0 / 10000.0 ** (np.arange(0, DH, 2, dtype=np.float32) / DH)).astype(np.float32)
    pos = np.arange(S, dtype=np.float32)
    freqs = (pos[:, None] * inv_freq[None, :]).astype(np.float32)
    c["cos_t"] = np.cos(freqs).astype(np.float32)
    c["sin_t"] = np.sin(freqs).astype(np.float32)
    return c


CONST_SPECS = {
    "ident": ([128, 128], F32, BF16),
    "blocktri": ([128, 128], F32, BF16),
    "e16": ([8, 128], F32, BF16), "e16t": ([128, 8], F32, BF16),
    "ones8": ([8, 1], F32, F32), "ones1x8": ([1, 8], F32, F32),
    "ones1x64": ([1, 64], F32, F32), "ones1x128": ([1, 128], F32, BF16),
    "tokid_u32": ([128, 256], U32, U32), "tok8k": ([128, 256], F32, F32),
    "basem1": ([128, 1], F32, F32), "bound": ([128, 1], F32, F32),
    "minus20": ([128, 1], F32, F32), "eps": ([128, 1], F32, F32),
    "padinit": ([128, 16], U32, U32),
}


class Ctx:
    def __init__(self, nc, es, block):
        self.nc, self.es, self.block = nc, es, block
        self.h, self.n = {}, {}

    def sem(self, name):
        if name not in self.h:
            self.h[name] = self.es.enter_context(self.nc.semaphore(name))
            self.n[name] = 0
        return self.h[name]

    def inc(self, inst, name, k=1):
        inst.then_inc(self.sem(name), k)
        self.n[name] += k
        return self.n[name]

    def val(self, name):
        self.sem(name)
        return self.n[name]


def build_program(parts=("ffn1", "attn", "ffn2")):
    nc = bass.Bass("TRN2", target_bir_lowering=False, debug=False)

    x_in = nc.dram_tensor("x", [S, D], F32, kind="ExternalInput").ap()
    wqkvo = {m: nc.dram_tensor(m, [D, D], F32, kind="ExternalInput").ap()
             for m in ("wq", "wk", "wv", "wo")}
    bqkvo = {m: nc.dram_tensor(m, [1, D], F32, kind="ExternalInput").ap()
             for m in ("bq", "bk", "bv", "bo")}
    gate_wT = nc.dram_tensor("gate_wT", [D, E], F32, kind="ExternalInput").ap()
    w1 = nc.dram_tensor("w1", [E, D, FF], F32, kind="ExternalInput").ap()
    w2 = nc.dram_tensor("w2", [E, FF, D], F32, kind="ExternalInput").ap()
    cdram = {k: nc.dram_tensor("c_" + k, sh, dt, kind="ExternalInput").ap()
             for k, (sh, dt, _) in CONST_SPECS.items()}
    for b in range(4):
        cdram[f"bandmask{b}"] = nc.dram_tensor(f"c_bandmask{b}", [128, 512], F32, kind="ExternalInput").ap()
    cdram["cos_t"] = nc.dram_tensor("c_cos_t", [S, 32], F32, kind="ExternalInput").ap()
    cdram["sin_t"] = nc.dram_tensor("c_sin_t", [S, 32], F32, kind="ExternalInput").ap()

    out = nc.dram_tensor("out", [S, D], F32, kind="ExternalOutput").ap()

    x_res = nc.dram_tensor("x_res", [SP, D], F32, kind="Internal").ap()
    xn_sp = nc.dram_tensor("xn_sp", [SP, D], BF16, kind="Internal").ap()
    probsT_d = nc.dram_tensor("probsT_d", [E, S], F32, kind="Internal").ap()
    scoresT_d = nc.dram_tensor("scoresT_d", [E, S], F32, kind="Internal").ap()
    idx_d = nc.dram_tensor("idx_d", [128, 256], U32, kind="Internal").ap()
    pay_d = nc.dram_tensor("pay_d", [128, 512], U32, kind="Internal").ap()
    m_g_d = nc.dram_tensor("m_g_d", [E * 1024, 2], U32, kind="Internal").ap()
    outT_sp = nc.dram_tensor("outT_sp", [8, 128, S], BF16, kind="Internal").ap()
    rl_d = nc.dram_tensor("rl_d", [1, 512], F32, kind="Internal").ap()

    es = ExitStack()
    block = es.enter_context(nc.Block())
    cx = Ctx(nc, es, block)

    csb = {}
    for k, (sh, dt, sbdt) in CONST_SPECS.items():
        csb[k] = es.enter_context(nc.sbuf_tensor("s_" + k, sh, sbdt))

    def _(sync):
        for k, (sh, dt, sbdt) in CONST_SPECS.items():
            if sbdt == dt:
                cx.inc(sync.dma_start(out=csb[k][:], in_=cdram[k]), "c_ld", 16)
    block.sync(_)

    def _(gp):
        for k, (sh, dt, sbdt) in CONST_SPECS.items():
            if sbdt != dt:
                cx.inc(gp.dma_start(out=csb[k][:], in_=cdram[k]), "c_ldg", 16)
    block.gpsimd(_)
    C_LD, C_LDG = cx.val("c_ld"), cx.val("c_ldg")

    finals = []

    def emit_ffn(t, res_in, res_scatter, copy_to_res, fence=None):
        ffn = ExitStack()
        sb = lambda name, sh, dt: ffn.enter_context(nc.sbuf_tensor(t + name, sh, dt))
        base = dict(cx.n)
        sv = lambda name: cx.val(name) - base.get(name, 0)

        def W(eng, name, v):
            eng.wait_ge(cx.sem(name), base.get(name, 0) + v)

        def FW(eng):
            if fence is not None:
                for fn, fv in fence:
                    eng.wait_ge(cx.sem(fn), fv)
        idx_e = [sb(f"_idxe{i}", [128, 32], U32) for i in range(2)]
        pay_e = [sb(f"_paye{i}", [128, 64], U32) for i in range(2)]
        mg_e = [sb(f"_mge{i}", [128, 16], U32) for i in range(2)]
        xin_st = sb("_xin", [128, 8 * D], BF16)
        xinT = sb("_xinT", [128, 8 * 1024], BF16)
        w1sb = [sb(f"_w1sb{i}", [128, 8 * 2048], BF16) for i in range(2)]
        hsb = sb("_hsb", [128, 32 * 1024], BF16)
        ysb = [sb(f"_ysb{i}", [128, 512], F32) for i in range(7)]
        tk2 = [sb(f"_tk2{i}", [128, 16], U32) for i in range(2)]

        p01 = ExitStack()
        sbp = lambda name, sh, dt: p01.enter_context(nc.sbuf_tensor(t + name, sh, dt))
        xb = [sbp(f"_xb{i}", [128, D], F32) for i in range(2)]
        stat = sbp("_stat", [128, 8], F32)
        xnb = [sbp(f"_xn{i}", [128, D], BF16) for i in range(2)]
        xnT_st = [sbp(f"_xnT{i}", [128, 128], BF16) for i in range(16)]
        gateTc = sbp("_gateTc", [128, 64], BF16)
        p2pack = sbp("_p2pack", [40, 512], F32)
        pexp = p2pack.ap()[0:8, :]
        sc_ch = p2pack.ap()[32:40, :]
        sc2 = sbp("_sc2", [8, 512], F32)
        probsT128 = sbp("_pT128", [128, 256], F32)
        lo, hi, mid = sbp("_lo", [128, 1], F32), sbp("_hi", [128, 1], F32), sbp("_mid", [128, 1], F32)
        geb = sbp("_geb", [128, 256], BF16)
        partial = sbp("_part", [128, 1], F32)
        partial_bf = sbp("_partbf", [128, 1], BF16)
        sel8bf = sbp("_sel8", [8, 1], BF16)
        selsb = sbp("_selsb", [128, 1], F32)

        idxu = sbp("_idxu", [128, 256], U32)
        pay = sbp("_pay", [128, 512], U32)
        rowbf = sbp("_rowbf", [128, 1], BF16)


        rsc = ExitStack()
        tmp1 = rsc.enter_context(nc.sbuf_tensor(t + "_tmp1", [128, 256], F32))
        tmp2 = rsc.enter_context(nc.sbuf_tensor(t + "_tmp2", [128, 256], F32))
        posf = rsc.enter_context(nc.sbuf_tensor(t + "_posf", [128, 256], F32))
        overm = rsc.enter_context(nc.sbuf_tensor(t + "_overm", [128, 256], F32))

        p0ps = ExitStack()
        ps_tr = [p0ps.enter_context(nc.psum_tensor(t + f"_pstr{i}", [128, 128], BF16)) for i in range(2)]
        ps_sc = [p0ps.enter_context(nc.psum_tensor(t + f"_pssc{i}", [8, 128], F32)) for i in range(2)]

        if copy_to_res:
            def _(gp):
                FW(gp)
                gp.memset(xb[0][:], 0.0)
                gp.memset(xnb[0][:], 0.0)
                gp.drain()
                cx.inc(gp.dma_start(out=res_scatter[S:SP, :], in_=xb[0][:]), "_zp", 16)
                cx.inc(gp.dma_start(out=xn_sp[S:SP, :], in_=xnb[0][:]), "_zp", 16)
            block.gpsimd(_)
        ZPV = sv("_zp")

        def _(sync):
            FW(sync)
            if ZPV:
                W(sync, "_zp", ZPV)
            for i in range(NT):
                if i >= 2:
                    W(sync, "_xbfree", i - 1)
                cx.inc(sync.dma_start(out=xb[i % 2][:], in_=res_in[i * 128:(i + 1) * 128, :]), f"_ldx{i % 2}", 16)
                if i >= 7 and (i - 7) % 4 == 0:
                    c = (i - 7) // 4
                    W(sync, "_sce", 4 * (c + 1))
                    cx.inc(sync.dma_start(out=scoresT_d[:, c * 512:(c + 1) * 512], in_=sc_ch[:]), "_scd", 16)
            for c in range(7, 8):
                W(sync, "_sce", 4 * (c + 1))
                cx.inc(sync.dma_start(out=scoresT_d[:, c * 512:(c + 1) * 512], in_=sc_ch[:]), "_scd", 16)
        block.sync(_)

        def emit_evacs(s, m):
            for dc in range(8):
                W(s, "_xTp", 8 * m + dc + 1)
                s.copy(out=xnT_st[(m % 2) * 8 + dc][:], in_=ps_tr[dc % 2][:])
                s.drain()
                cx.inc(s.nop(), "_xTe", 1)

        def _(s):
            FW(s)
            s.wait_ge(cx.sem("c_ld"), C_LD)
            if ZPV:
                W(s, "_zp", ZPV)
            for i in range(NT):
                if i >= 2:
                    emit_evacs(s, i - 2)
                W(s, f"_ldx{i % 2}", 16 * (i // 2 + 1))
                if i >= 2:
                    W(s, f"_xnsp{i % 2}", 16 * (i // 2))
                    W(s, "_xTp", 8 * (i - 1))
                s.activation(out=xnb[i % 2][:], in_=xb[i % 2][:], func=AF.Square, accum_out=stat[:, 6:7])
                s.drain()
                cx.inc(s.nop(), "_sq", 1)
                W(s, "_var", i + 1)
                s.activation(out=stat[:, 3:4], in_=stat[:, 2:3], func=AF.Sqrt, bias=csb["eps"][:])
                s.drain()
                cx.inc(s.nop(), "_sd", 1)
            emit_evacs(s, NT - 2)
            emit_evacs(s, NT - 1)
        block.scalar(_)

        def _(v):
            FW(v)
            def sc_evac(m):
                if m < 0 or m >= NT:
                    return
                W(v, "_scp", m + 1)
                if m % 4 == 0 and m >= 4:
                    W(v, "_scd", 16 * (m // 4))
                v.tensor_copy(sc_ch[:, (m % 4) * 128:(m % 4 + 1) * 128], ps_sc[m % 2][:])
                v.drain()
                cx.inc(v.nop(), "_sce", 1)
            v.wait_ge(cx.sem("c_ld"), C_LD)
            for i in range(NT):
                sc_evac(i - 3)
                W(v, f"_ldx{i % 2}", 16 * (i // 2 + 1))
                v.tensor_reduce(out=stat[:, 0:1], in_=xb[i % 2][:], axis=AX, op=ALU.add)
                v.drain()
                v.tensor_scalar(out=stat[:, 0:1], in0=stat[:, 0:1], scalar1=1.0 / D, scalar2=None, op0=ALU.mult)
                W(v, "_sq", i + 1)
                v.tensor_scalar(out=stat[:, 1:2], in0=stat[:, 6:7], scalar1=1.0 / D, scalar2=None, op0=ALU.mult)
                v.drain()
                v.tensor_tensor(out=stat[:, 4:5], in0=stat[:, 0:1], in1=stat[:, 0:1], op=ALU.mult)
                v.drain()
                v.tensor_tensor(out=stat[:, 2:3], in0=stat[:, 1:2], in1=stat[:, 4:5], op=ALU.subtract)
                v.drain()
                cx.inc(v.nop(), "_var", 1)
                W(v, "_sd", i + 1)
                v.reciprocal(out=stat[:, 5:6], in_=stat[:, 3:4])
                v.drain()
                v.tensor_scalar(out=xnb[i % 2][:], in0=xb[i % 2][:], scalar1=stat[:, 0:1], scalar2=stat[:, 5:6], op0=ALU.subtract, op1=ALU.mult)
                v.drain()
                cx.inc(v.nop(), "_xn", 1)
            sc_evac(NT - 3)
            sc_evac(NT - 2)
            sc_evac(NT - 1)
        block.vector(_)

        def _(gp):
            FW(gp)
            cx.inc(gp.dma_start(out=gateTc[:].rearrange("p (c e) -> p c e", c=8), in_=gate_wT.rearrange("(c p) e -> p c e", p=128)), "_gTe", 16)
            for i in range(NT):
                W(gp, "_xn", i + 1)
                cx.inc(gp.dma_start(out=xn_sp[i * 128:(i + 1) * 128, :], in_=xnb[i % 2][:]), f"_xnsp{i % 2}", 16)
                if copy_to_res:
                    cx.inc(gp.dma_start(out=res_scatter[i * 128:(i + 1) * 128, :], in_=xb[i % 2][:]), f"_rescp{i % 2}", 16)
                    W(gp, f"_rescp{i % 2}", 16 * (i // 2 + 1))
                cx.inc(gp.nop(), "_xbfree", 1)
        block.gpsimd(_)

        def _(tn):
            FW(tn)
            tn.wait_ge(cx.sem("c_ldg"), C_LDG)
            W(tn, "_gTe", 16)
            for i in range(NT):
                W(tn, "_xn", i + 1)
                base = 16 * i
                for dc in range(8):
                    k = base + dc
                    if k >= 2:
                        W(tn, "_xTe", sv("_xTp") - 1)
                    cx.inc(tn.transpose(ps_tr[dc % 2][:], xnb[i % 2][:, dc * 128:(dc + 1) * 128], csb["ident"][:]), "_xTp", 1)
                W(tn, "_xTe", sv("_xTp"))
                if i >= 1:
                    W(tn, "_sce", i)
                for dc in range(8):
                    inst = tn.matmul(ps_sc[i % 2][:], gateTc[:, dc * 8:(dc + 1) * 8], xnT_st[(i % 2) * 8 + dc][:],
                                     start=(dc == 0), stop=(dc == 7))
                cx.inc(inst, "_scp", 1)
        block.tensor(_)





        ps_s1 = p0ps.enter_context(nc.psum_tensor(t + "_pss1", [1, 512], F32))
        ps_pb = p0ps.enter_context(nc.psum_tensor(t + "_pspb", [8, 512], F32))
        ps_small = p0ps.enter_context(nc.psum_tensor(t + "_pssm", [128, 512], F32))
        ps_c = ps_small.ap()[0:8, 0:1]
        ps_c2 = ps_small.ap()[:, 1:2]
        ps_co = ps_small.ap()[:, 2:3]

        def _(sync):
            FW(sync)
            for c in range(8):
                if c >= 1:
                    W(sync, "_probc", c)
                    cx.inc(sync.dma_start(out=probsT_d[:, (c - 1) * 512:c * 512], in_=pexp[:]), "_pmul", 16)
                W(sync, "_scd", 16 * (c + 1))
                if c >= 1:
                    W(sync, "_pbc", c)
                cx.inc(sync.dma_start(out=sc2[:], in_=scoresT_d[:, c * 512:(c + 1) * 512]), "_sc2", 16)
            W(sync, "_probc", 8)
            cx.inc(sync.dma_start(out=probsT_d[:, 7 * 512:8 * 512], in_=pexp[:]), "_pmul", 16)
            W(sync, "_pmul", sv("_pmul"))
            cx.inc(sync.dma_start(out=probsT128[:], in_=probsT_d.rearrange("e (j f) -> (e j) f", f=256)), "_p128", 16)
        block.sync(_)

        def _(s):
            FW(s)
            for c in range(8):
                W(s, "_sc2", 16 * (c + 1))
                if c >= 1:
                    W(s, "_pmul", 16 * c)
                s.activation(out=pexp[:], in_=sc2[:], func=AF.Exp)
                s.drain()
                cx.inc(s.nop(), "_exp", 1)
        block.scalar(_)

        def _(tn):
            FW(tn)
            for c in range(8):
                W(tn, "_exp", c + 1)
                cx.inc(tn.matmul(ps_s1[:], csb["ones8"][:], pexp[:], start=True, stop=True), "_sum", 1)
                W(tn, "_rcpv", c + 1)
                cx.inc(tn.matmul(ps_pb[:], csb["ones1x8"][:], sc2[0:1, :], start=True, stop=True), "_pbc", 1)
        block.tensor(_)

        def _(v):
            FW(v)
            for c in range(8):
                W(v, "_sum", c + 1)
                v.reciprocal(out=sc2[0:1, :], in_=ps_s1[:])
                v.drain()
                cx.inc(v.nop(), "_rcpv", 1)
                W(v, "_pbc", c + 1)
                v.tensor_tensor(out=pexp[:], in0=pexp[:], in1=ps_pb[:], op=ALU.mult)
                v.drain()
                cx.inc(v.nop(), "_probc", 1)
        block.vector(_)



        def _(gp):
            FW(gp)
            gp.memset(lo[:], 0.0)
            gp.memset(hi[:], 1.0)
            gp.drain()
            cx.inc(gp.nop(), "_bi0", 1)
        block.gpsimd(_)

        def _(v):
            FW(v)
            W(v, "_p128", sv("_p128"))
            W(v, "_bi0", 1)
            for it in range(BISECT):
                v.tensor_scalar(out=mid[:], in0=lo[:], scalar1=hi[:], scalar2=0.5, op0=ALU.add, op1=ALU.mult)
                v.drain()
                v.tensor_scalar(out=geb[:], in0=probsT128[:], scalar1=mid[:], scalar2=None, op0=ALU.is_ge)
                v.drain()
                v.tensor_reduce(out=partial[:], in_=geb[:], axis=AX, op=ALU.add)
                v.drain()
                v.tensor_copy(partial_bf[:], partial[:])
                v.drain()
                cx.inc(v.nop(), "_biv", 1)
                W(v, "_bic", it + 1)
                v.tensor_scalar(out=sel8bf[:], in0=ps_c, scalar1=1024.0, scalar2=None, op0=ALU.is_ge)
                v.drain()
                cx.inc(v.nop(), "_bis", 1)
                W(v, "_bib", it + 1)
                v.tensor_copy(selsb[:], ps_c2)
                v.drain()
                v.tensor_tensor(out=tmp1[:, 0:1], in0=mid[:], in1=lo[:], op=ALU.subtract)
                v.tensor_tensor(out=tmp1[:, 1:2], in0=hi[:], in1=mid[:], op=ALU.subtract)
                v.drain()
                v.tensor_scalar(out=tmp1[:, 0:2], in0=tmp1[:, 0:2], scalar1=selsb[:], scalar2=None, op0=ALU.mult)
                v.drain()
                v.tensor_tensor(out=lo[:], in0=lo[:], in1=tmp1[:, 0:1], op=ALU.add)
                v.tensor_tensor(out=hi[:], in0=mid[:], in1=tmp1[:, 1:2], op=ALU.add)
                v.drain()
            v.tensor_scalar(out=geb[:], in0=probsT128[:], scalar1=lo[:], scalar2=None, op0=ALU.is_ge)
            v.drain()
            v.tensor_copy(tmp1[:], geb[:])
            v.drain()
            cur, nxt = tmp1, tmp2
            for sft in [1, 2, 4, 8, 16, 32, 64, 128]:
                v.tensor_copy(nxt[:, 0:sft], cur[:, 0:sft])
                v.tensor_tensor(out=nxt[:, sft:256], in0=cur[:, sft:256], in1=cur[:, 0:256 - sft], op=ALU.add)
                v.drain()
                cur, nxt = nxt, cur
            v.tensor_copy(rowbf[:], cur[:, 255:256])
            v.drain()
            cx.inc(v.nop(), "_rowc", 1)
            W(v, "_coar", 1)
            v.tensor_scalar(out=posf[:], in0=cur[:], scalar1=ps_co, scalar2=csb["basem1"][:], op0=ALU.add, op1=ALU.add)
            v.drain()
            v.select(out=tmp2[:], mask=geb.ap().bitcast(mybir.dt.int16), on_true=posf[:], on_false=csb["tok8k"][:], add_drain=True)
            v.drain()
            v.tensor_scalar(out=overm[:], in0=tmp2[:], scalar1=csb["bound"][:], scalar2=None, op0=ALU.is_gt)
            v.drain()
            v.select(out=posf[:], mask=overm.ap().bitcast(I32), on_true=csb["tok8k"][:], on_false=tmp2[:], add_drain=True)
            v.drain()
            v.tensor_copy(idxu[:], posf[:])
            v.drain()
            v.tensor_copy(pay[:, 0:512:2], csb["tokid_u32"][:])
            v.drain()
            v.tensor_scalar(out=overm[:], in0=probsT128[:], scalar1=0.5, scalar2=None, op0=ALU.mult)
            v.drain()
            v.tensor_copy(pay.ap().bitcast(F32)[:, 1:512:2], overm[:])
            v.drain()
            cx.inc(v.nop(), "_route", 1)
        block.vector(_)

        def _(tn):
            FW(tn)
            for it in range(BISECT):
                W(tn, "_biv", it + 1)
                cx.inc(tn.matmul(ps_c, csb["e16t"][:], partial_bf[:], start=True, stop=True), "_bic", 1)
                W(tn, "_bis", it + 1)
                cx.inc(tn.matmul(ps_c2, csb["e16"][:], sel8bf[:], start=True, stop=True), "_bib", 1)
            W(tn, "_rowc", 1)
            cx.inc(tn.matmul(ps_co, csb["blocktri"][:], rowbf[:], start=True, stop=True), "_coar", 1)
        block.tensor(_)

        def _(sync):
            FW(sync)
            W(sync, "_route", 1)
            cx.inc(sync.dma_start(out=idx_d, in_=idxu[:]), "_ipd", 16)
            cx.inc(sync.dma_start(out=pay_d, in_=pay[:]), "_ipd", 16)
            W(sync, "_ipd", sv("_ipd"))
            for e in range(E):
                if e >= 2:
                    W(sync, "_invdone", e - 1)
                src_i = idx_d[16 * e:16 * e + 16, :].rearrange("r (c8 k) -> (r c8) k", c8=8)
                cx.inc(sync.dma_start(out=idx_e[e % 2][:], in_=src_i), f"_ier{e % 2}", 16)
                src_p = pay_d[16 * e:16 * e + 16, :].rearrange("r (c8 k) -> (r c8) k", c8=8)
                cx.inc(sync.dma_start(out=pay_e[e % 2][:], in_=src_p), f"_ier{e % 2}", 16)
                W(sync, "_invdone", e + 1)
                if e >= 2:
                    W(sync, "_mgfree", e - 1)
                src = m_g_d[e * 1024:(e + 1) * 1024, :].rearrange("(p j) c -> p (j c)", p=128)
                cx.inc(sync.dma_start(out=mg_e[e % 2][:], in_=src), f"_mgr{e % 2}", 16)
        block.sync(_)

        rsc.close()
        p01.close()
        w2sb = [ffn.enter_context(nc.sbuf_tensor(t + f"_w2sb{i}", [128, 512], BF16)) for i in range(7)]
        p0ps.close()
        p2ps = ExitStack()
        pool = [p2ps.enter_context(nc.psum_tensor(t + f"_pool{i}", [128, 1024], F32)) for i in range(4)]
        trps = [pool[2].ap().bitcast(BF16)[:, 0:128], pool[3].ap().bitcast(BF16)[:, 0:128]]

        def _(gp):
            FW(gp)
            breg = gp.to_reg(8191)
            W(gp, "_route", 1)
            for e in range(E):
                cx.inc(gp.dma_start(out=m_g_d[e * 1024:(e + 1) * 1024, :].rearrange("(p j) c -> p (j c)", p=128),
                                    in_=csb["padinit"][:]), "_mgi", 16)
            W(gp, "_mgi", sv("_mgi"))
            W(gp, "_xnsp0", 16 * NT // 2)
            W(gp, "_xnsp1", 16 * NT // 2)
            def inv_gath_w1A_thunks(e):
                th = [lambda: W(gp, f"_ier{e % 2}", 32 * (e // 2 + 1))]

                def mk_scat(jj):
                    def f():
                        cx.inc(gp.indirect_dma_start(
                            out=m_g_d,
                            out_offset=IndirectOffsetOnAxis(ap=idx_e[e % 2][:, jj:jj + 1], axis=0),
                            in_=pay_e[e % 2][:, 2 * jj:2 * jj + 2],
                            in_offset=None, bounds_check=breg, oob_is_err=False), "_inv", 16)
                    return f
                th += [mk_scat(jj) for jj in range(32)]

                def f_done():
                    W(gp, "_inv", 16 * 32 * (e + 1))
                    cx.inc(gp.nop(), "_invdone", 1)
                    W(gp, f"_mgr{e % 2}", 16 * (e // 2 + 1))
                    if e >= 1:
                        W(gp, "_xinfree", e)
                th.append(f_done)
                mgx = mg_e[e % 2][:].rearrange("p (j c) -> p j c", c=2)

                def mk_gath(j):
                    def f():
                        cx.inc(gp.indirect_dma_start(
                            out=xin_st[:, j * D:(j + 1) * D],
                            out_offset=None, in_=xn_sp,
                            in_offset=IndirectOffsetOnAxis(ap=mgx[:, j:j + 1, 0:1], axis=0)), "_gat", 16)
                    return f
                th += [mk_gath(j) for j in range(8)]

                def f_w1w():
                    if e >= 1:
                        W(gp, "_hp", 32 * (e - 1) + 16)
                th.append(f_w1w)

                def mk_w1(dc):
                    def f():
                        cx.inc(gp.dma_start(out=w1sb[0][:, dc * 2048:(dc + 1) * 2048], in_=w1[e, dc * 128:(dc + 1) * 128, 0:2048]), "_w1lA", 16)
                    return f
                th += [mk_w1(dc) for dc in range(8)]
                return th

            th0 = inv_gath_w1A_thunks(0)
            for f in th0[-8:]:
                f()
            for f in th0[:-8]:
                f()
            for e in range(E):
                mg = mg_e[e % 2][:].rearrange("p (j c) -> p j c", c=2)

                def w2load(gfc):
                    half, fc = gfc // 32, gfc % 32
                    gn = 64 * e + gfc
                    nload = sum(sv(f"_w2l{r}") // 16 for r in range(7))
                    if nload == 0:
                        W(gp, "_route", 1)
                    if nload >= 7:
                        W(gp, "_w2free", nload - 6)
                    cx.inc(gp.dma_start(out=w2sb[gn % 7][:],
                                        in_=w2[e, fc * 128:(fc + 1) * 128, half * 512:(half + 1) * 512]), f"_w2l{gn % 7}", 16)

                H1POS = {4: 0, 5: 1, 6: 2, 7: 3, 0: 4, 1: 5, 2: 6, 3: 7}

                def scat(half, j):
                    pos = j if half == 0 else H1POS[j]
                    gj8 = 16 * e + 8 * half + pos
                    W(gp, "_yev", gj8 + 1)
                    if half == 0 and j == 0:
                        W(gp, "_tk2", e + 1)
                    W(gp, "_sca", sv("_sca"))
                    cx.inc(gp.indirect_dma_start(
                        out=res_scatter.rearrange("s (h c) -> (s h) c", h=2),
                        out_offset=IndirectOffsetOnAxis(ap=tk2[e % 2][:, 2 * j + half:2 * j + half + 1], axis=0),
                        in_=ysb[gj8 % 7][:], in_offset=None, compute_op=ALU.add), "_sca", 16)

                if e >= 1:
                    W(gp, "_hp", 32 * e)
                for dc in range(8):
                    cx.inc(gp.dma_start(out=w1sb[1][:, dc * 2048:(dc + 1) * 2048], in_=w1[e, dc * 128:(dc + 1) * 128, 2048:4096]), "_w1lB", 16)
                for gfc in range(7):
                    w2load(gfc)
                for f in (inv_gath_w1A_thunks(e + 1) if e + 1 < E else []):
                    f()
                for gfc in range(7, 64):
                    if gfc == 36:
                        scat(0, 0)
                    elif 40 <= gfc <= 60 and gfc % 4 == 0:
                        scat(0, (gfc - 36) // 4)
                    elif gfc == 62:
                        scat(0, 7)
                    w2load(gfc)
                for j in (4, 5, 6, 7, 0, 1, 2, 3):
                    scat(1, j)
                W(gp, "_sca", sv("_sca"))
                cx.inc(gp.nop(), "_mgfree", 1)
        block.gpsimd(_)

        def _(tn):
            FW(tn)
            for e in range(E):
                W(tn, "_gat", 16 * 8 * (e + 1))
                if e >= 1:
                    W(tn, "_yev", 16 * e - 4)
                for j in range(8):
                    for dc in range(8):
                        k = j * 8 + dc
                        if sv("_xiTp") >= 2:
                            W(tn, "_xiTe", sv("_xiTp") - 1)
                        cx.inc(tn.transpose(trps[k % 2], xin_st[:, j * D + dc * 128:j * D + (dc + 1) * 128], csb["ident"][:]), "_xiTp", 1)
                W(tn, "_xiTe", sv("_xiTp"))
                cx.inc(tn.nop(), "_xinfree", 1)
                W(tn, "_w1lA", 16 * 8 * (e + 1))
                for fh in range(2):
                    if fh == 1:
                        W(tn, "_w1lB", 16 * 8 * (e + 1))
                    for fcl in range(16):
                        fc = fh * 16 + fcl
                        if e >= 1 and fc == 0:
                            W(tn, "_yev", 16 * e - 2)
                        if e >= 1 and fc == 1:
                            W(tn, "_yev", 16 * e)
                        if fc >= 2:
                            W(tn, "_gel", 32 * e + fc - 1)
                        for dc in range(8):
                            lhs = w1sb[fh][:, dc * 2048 + fcl * 128:dc * 2048 + (fcl + 1) * 128]
                            tn.matmul(pool[fc % 2].ap()[:, 0:512], lhs, xinT[:, dc * 1024:dc * 1024 + 512], start=(dc == 0), stop=(dc == 7))
                            inst = tn.matmul(pool[fc % 2].ap()[:, 512:1024], lhs, xinT[:, dc * 1024 + 512:(dc + 1) * 1024], start=(dc == 0), stop=(dc == 7))
                        cx.inc(inst, "_hp", 1)
                W(tn, "_gel", 32 * (e + 1))
                for half in range(2):
                    for fc in range(32):
                        gn = 64 * e + half * 32 + fc
                        W(tn, f"_w2l{gn % 7}", 16 * (gn // 7 + 1))
                        for j in range(8):
                            if fc == 0 and half == 1:
                                W(tn, "_yev", 16 * e + j + 1)
                            lhs = hsb[:, fc * 1024 + j * 128:fc * 1024 + (j + 1) * 128]
                            inst = tn.matmul(pool[j // 2].ap()[:, (j % 2) * 512:(j % 2 + 1) * 512], lhs,
                                             w2sb[gn % 7][:], start=(fc == 0), stop=(fc == 31))
                        cx.inc(inst, "_w2free", 1)
        block.tensor(_)

        def _(s):
            FW(s)
            for e in range(E):
                for j in range(8):
                    for dc in range(8):
                        k = j * 8 + dc
                        W(s, "_xiTp", 64 * e + k + 1)
                        s.copy(out=xinT[:, dc * 1024 + j * 128:dc * 1024 + (j + 1) * 128], in_=trps[k % 2])
                        s.drain()
                        cx.inc(s.nop(), "_xiTe", 1)
                for fc in range(32):
                    W(s, "_hp", 32 * e + fc + 1)
                    s.activation(out=hsb[:, fc * 1024:(fc + 1) * 1024], in_=pool[fc % 2].ap()[:], func=AF.Gelu)
                    s.drain()
                    cx.inc(s.nop(), "_gel", 1)
        block.scalar(_)

        def _(v):
            FW(v)
            for e in range(E):
                mgf = mg_e[e % 2][:].rearrange("p (j c) -> p j c", c=2).bitcast(F32)
                if e >= 2:
                    W(v, "_sca", 256 * (e - 1))
                W(v, f"_mgr{e % 2}", 16 * (e // 2 + 1))
                mgtok = mg_e[e % 2][:].rearrange("p (j c) -> p j c", c=2)[:, :, 0:1].squeeze(2)
                tk2v = tk2[e % 2][:].rearrange("p (j two) -> p j two", two=2)
                v.tensor_scalar(out=tk2v[:, :, 0:1].squeeze(2), in0=mgtok, scalar1=1, scalar2=None, op0=ALU.logical_shift_left)
                v.tensor_scalar(out=tk2v[:, :, 1:2].squeeze(2), in0=mgtok, scalar1=1, scalar2=1, op0=ALU.logical_shift_left, op1=ALU.bitwise_or)
                v.drain()
                cx.inc(v.nop(), "_tk2", 1)
                for half in range(2):
                    W(v, "_w2free", 64 * e + 32 * (half + 1))
                    jorder = range(8) if half == 0 else (4, 5, 6, 7, 0, 1, 2, 3)
                    for pos, j in enumerate(jorder):
                        gj8 = 16 * e + 8 * half + pos
                        if gj8 >= 7:
                            W(v, "_sca", 16 * (gj8 - 6))
                        v.tensor_scalar(out=ysb[gj8 % 7][:], in0=pool[j // 2].ap()[:, (j % 2) * 512:(j % 2 + 1) * 512],
                                        scalar1=mgf[:, j:j + 1, 1:2].squeeze(1), scalar2=None, op0=ALU.mult)
                        v.drain()
                        cx.inc(v.nop(), "_yev", 1)
        block.vector(_)

        finals.append(("_sca", cx.val("_sca")))
        p2ps.close()
        ffn.close()


    def emit_attn(t, fence):
        att = ExitStack()
        sb = lambda name, sh, dt: att.enter_context(nc.sbuf_tensor(t + name, sh, dt))
        fs_name, fs_val = fence
        base = dict(cx.n)
        sv = lambda name: cx.val(name) - base.get(name, 0)

        def W(eng, name, v):
            eng.wait_ge(cx.sem(name), base.get(name, 0) + v)

        xb = [sb(f"_xb{i}", [128, D], F32) for i in range(2)]
        stat = sb("_stat", [128, 8], F32)
        xnb = [sb(f"_xn{i}", [128, D], BF16) for i in range(2)]
        xn1T = sb("_xn1T", [128, 8 * S], BF16)
        cos_sb = sb("_cos", [128, NT * 32], F32)
        sin_sb = sb("_sin", [128, NT * 32], F32)
        bmask = [sb(f"_bmk{i}", [128, 512], F32) for i in range(4)]
        wqk_sb = [sb(f"_wqk{i}", [128, 8 * 256], BF16) for i in range(2)]
        bqk_sb = [sb(f"_bqk{i}", [1, 256], BF16) for i in range(2)]
        wv_sb = [sb(f"_wv{i}", [128, 8 * 128], BF16) for i in range(2)]
        bv_sb = [sb(f"_bv{i}", [1, 128], BF16) for i in range(2)]
        QT = [sb(f"_QT{i}", [128, S], BF16) for i in range(2)]
        KT = [sb(f"_KT{i}", [128, S], BF16) for i in range(2)]
        VT = [sb(f"_VT{i}", [128, NT * 130], BF16) for i in range(2)]
        wo_bf = sb("_wo", [128, 8 * 1024], BF16)
        rot = sb("_rot", [128, 256], F32)
        rtm = sb("_rtm", [128, 256], F32)
        qkbf = sb("_qkbf", [128, 256], BF16)
        PT = [sb(f"_PT{i}", [128, 1024], BF16) for i in range(2)]
        oT2 = [sb(f"_oT{i}", [65, 512], F32) for i in range(2)]
        rl = sb("_rl", [1, 512], F32)
        lnt = sb("_lnt", [1, 512], F32)
        rlb = sb("_rlb", [64, 512], F32)
        outT = sb("_outT", [128, S], BF16)
        bo_sb = sb("_bo", [1, D], BF16)

        aps = ExitStack()
        ps_qk = aps.enter_context(nc.psum_tensor(t + "_psqk", [128, 512], F32))
        ps_vb = aps.enter_context(nc.psum_tensor(t + "_psvb", [128, 128], F32))
        ps_q = ps_qk.ap()[:, 0:128]
        ps_k = ps_qk.ap()[:, 128:256]
        ps_v = ps_vb.ap()[:, 0:128]
        ps_s2 = [aps.enter_context(nc.psum_tensor(t + f"_pss{i}", [128, 1024], F32)) for i in range(2)]
        ps_ob = aps.enter_context(nc.psum_tensor(t + "_pso", [128, 512], F32))
        ps_o = ps_ob.ap()
        ps_tr = aps.enter_context(nc.psum_tensor(t + "_pstr", [128, 512], BF16))
        qtr0 = ps_tr.ap()[:, 0:128]
        qtrk = ps_tr.ap()[:, 256:384]
        qtr = [qtr0, ps_s2[0].ap().bitcast(BF16)[:, 0:128]]

        def _(sync):
            sync.wait_ge(cx.sem(fs_name), fs_val)
            cx.inc(sync.dma_start(out=cos_sb[:].rearrange("p (i c) -> p i c", c=32), in_=cdram["cos_t"].rearrange("(i p) c -> p i c", p=128)), "_tb", 16)
            cx.inc(sync.dma_start(out=sin_sb[:].rearrange("p (i c) -> p i c", c=32), in_=cdram["sin_t"].rearrange("(i p) c -> p i c", p=128)), "_tb", 16)
            for b4 in range(4):
                cx.inc(sync.dma_start(out=bmask[b4][:], in_=cdram[f"bandmask{b4}"]), "_tb", 16)
            for i in range(NT):
                if i >= 2:
                    W(sync, "_xbfree", i - 1)
                cx.inc(sync.dma_start(out=xb[i % 2][:], in_=x_res[i * 128:(i + 1) * 128, :]), f"_ldx{i % 2}", 16)
        block.sync(_)
        TBV = sv("_tb")

        def _(s):
            s.wait_ge(cx.sem(fs_name), fs_val)
            s.wait_ge(cx.sem("c_ld"), C_LD)
            for i in range(NT):
                if i >= 1:
                    for dc in range(8):
                        k = 8 * (i - 1) + dc
                        W(s, "_xTp", k + 1)
                        s.copy(out=xn1T[:, dc * S + (i - 1) * 128:dc * S + i * 128], in_=qtr[k % 2])
                        s.drain()
                        cx.inc(s.nop(), "_xTe", 1)
                W(s, f"_ldx{i % 2}", 16 * (i // 2 + 1))
                if i >= 2:
                    W(s, "_xTp", 8 * (i - 1))
                s.activation(out=xnb[i % 2][:], in_=xb[i % 2][:], func=AF.Square, accum_out=stat[:, 6:7])
                s.drain()
                cx.inc(s.nop(), "_sq", 1)
                W(s, "_var", i + 1)
                s.activation(out=stat[:, 3:4], in_=stat[:, 2:3], func=AF.Sqrt, bias=csb["eps"][:])
                s.drain()
                cx.inc(s.nop(), "_sd", 1)
            for dc in range(8):
                k = 8 * (NT - 1) + dc
                W(s, "_xTp", k + 1)
                s.copy(out=xn1T[:, dc * S + (NT - 1) * 128:dc * S + NT * 128], in_=qtr[k % 2])
                s.drain()
                cx.inc(s.nop(), "_xTe", 1)
        block.scalar(_)

        def _(v):
            v.wait_ge(cx.sem(fs_name), fs_val)
            v.memset(VT[0][:], 1.0)
            v.memset(VT[1][:], 1.0)
            v.drain()
            cx.inc(v.nop(), "_vtm", 1)
            for i in range(NT):
                W(v, f"_ldx{i % 2}", 16 * (i // 2 + 1))
                v.tensor_reduce(out=stat[:, 0:1], in_=xb[i % 2][:], axis=AX, op=ALU.add)
                v.drain()
                v.tensor_scalar(out=stat[:, 0:1], in0=stat[:, 0:1], scalar1=1.0 / D, scalar2=None, op0=ALU.mult)
                W(v, "_sq", i + 1)
                v.tensor_scalar(out=stat[:, 1:2], in0=stat[:, 6:7], scalar1=1.0 / D, scalar2=None, op0=ALU.mult)
                v.drain()
                v.tensor_tensor(out=stat[:, 4:5], in0=stat[:, 0:1], in1=stat[:, 0:1], op=ALU.mult)
                v.drain()
                v.tensor_tensor(out=stat[:, 2:3], in0=stat[:, 1:2], in1=stat[:, 4:5], op=ALU.subtract)
                v.drain()
                cx.inc(v.nop(), "_var", 1)
                W(v, "_sd", i + 1)
                v.reciprocal(out=stat[:, 5:6], in_=stat[:, 3:4])
                v.drain()
                v.tensor_scalar(out=xnb[i % 2][:], in0=xb[i % 2][:], scalar1=stat[:, 0:1], scalar2=stat[:, 5:6], op0=ALU.subtract, op1=ALU.mult)
                v.drain()
                cx.inc(v.nop(), "_xn", 1)
        block.vector(_)

        def _(tn):
            tn.wait_ge(cx.sem(fs_name), fs_val)
            tn.wait_ge(cx.sem("c_ldg"), C_LDG)
            for i in range(NT):
                W(tn, "_xn", i + 1)
                for dc in range(8):
                    k = 8 * i + dc
                    if k >= 2:
                        W(tn, "_xTe", k - 1)
                    cx.inc(tn.transpose(qtr[k % 2], xnb[i % 2][:, dc * 128:(dc + 1) * 128], csb["ident"][:]), "_xTp", 1)
        block.tensor(_)

        def _(gp):
            gp.wait_ge(cx.sem(fs_name), fs_val)
            for i in range(NT):
                W(gp, "_xn", i + 1)
                cx.inc(gp.nop(), "_xbfree", 1)
            def rl_chain(P):
                W(gp, "_rc", P + 1)
                cx.inc(gp.dma_start(out=rl_d, in_=rl[:]), "_rlw", 16)
                W(gp, "_rlw", 16 * (P + 1))
                if P >= 1:
                    W(gp, "_om", P)
                cx.inc(gp.dma_start(out=rlb[:], in_=rl_d.broadcast_to((64, 512))), "_rlb", 16)

            for p in range(8):
                if p >= 2:
                    W(gp, "_vp", 32 * (p - 1))
                wqk2 = wqk_sb[p % 2][:].rearrange("p (c two n) -> p c two n", c=8, two=2)
                for two, m in ((0, "q"), (1, "k")):
                    cx.inc(gp.dma_start(
                        out=wqk2[:, :, two, :],
                        in_=wqkvo["w" + m].rearrange("(c p) n -> p c n", p=128)[:, :, 128 * p:128 * (p + 1)]), f"_wl{p % 2}", 16)
                    cx.inc(gp.dma_start(out=bqk_sb[p % 2][:, 128 * two:128 * (two + 1)], in_=bqkvo["b" + m][:, 128 * p:128 * (p + 1)]), f"_wl{p % 2}", 16)
                cx.inc(gp.dma_start(
                    out=wv_sb[p % 2][:].rearrange("p (c n) -> p c n", c=8),
                    in_=wqkvo["wv"].rearrange("(c p) n -> p c n", p=128)[:, :, 128 * p:128 * (p + 1)]), f"_wl{p % 2}", 16)
                cx.inc(gp.dma_start(out=bv_sb[p % 2][:], in_=bqkvo["bv"][:, 128 * p:128 * (p + 1)]), f"_wl{p % 2}", 16)
                if p >= 1:
                    for U in range(16):
                        rl_chain(16 * (p - 1) + U)
            for U in range(16):
                rl_chain(112 + U)
        block.gpsimd(_)

        NU = 16
        npr_of = lambda U: 2 * (U % 8) + 2
        pb = [0]
        for p in range(8):
            for U in range(NU):
                pb.append(pb[-1] + npr_of(U))

        def gen_tile(tn, p, u):
            gi = 32 * p + u
            if p == 0:
                W(tn, "_xTe", 8 * (u + 1))
            if gi >= 1:
                W(tn, "_ro", gi)
                W(tn, "_vev", gi)
            for dc in range(8):
                lhs = xn1T[:, dc * S + u * 128:dc * S + (u + 1) * 128]
                tn.matmul(ps_qk.ap()[:, 0:256], lhs, wqk_sb[p % 2][:, dc * 256:(dc + 1) * 256], start=(dc == 0), stop=False)
            cx.inc(tn.matmul(ps_qk.ap()[:, 0:256], csb["ones1x128"][:], bqk_sb[p % 2][:], start=False, stop=True), "_qkp", 1)
            if u >= 1:
                g = gi - 1
                W(tn, "_qTe", 2 * g)
                W(tn, "_roc", g + 1)
                cx.inc(tn.transpose(qtr0, qkbf[:, 0:128], csb["ident"][:]), "_qTp", 1)
            for dc in range(8):
                lhs = xn1T[:, dc * S + u * 128:dc * S + (u + 1) * 128]
                tn.matmul(ps_v, lhs, wv_sb[p % 2][:, dc * 128:(dc + 1) * 128], start=(dc == 0), stop=False)
            cx.inc(tn.matmul(ps_v, csb["ones1x128"][:], bv_sb[p % 2][:], start=False, stop=True), "_vp", 1)
            if u >= 1:
                g = gi - 1
                W(tn, "_qTe", 2 * g + 1)
                cx.inc(tn.transpose(qtrk, qkbf[:, 128:256], csb["ident"][:]), "_qTp", 1)

        def gen_tail(tn, p):
            g = 32 * p + NT - 1
            W(tn, "_qTe", 2 * g)
            W(tn, "_roc", g + 1)
            cx.inc(tn.transpose(qtr0, qkbf[:, 0:128], csb["ident"][:]), "_qTp", 1)
            W(tn, "_qTe", 2 * g + 1)
            cx.inc(tn.transpose(qtrk, qkbf[:, 128:256], csb["ident"][:]), "_qTp", 1)
            W(tn, "_qTe", 2 * g + 2)

        def pv_pair(tn, p, P, h, g0, j, npr):
            g = g0 + j
            par = g % 2
            W(tn, "_ex", g + 1)
            if j == 0 and P >= 1:
                W(tn, "_oevv", P)
            for half, kt in ((0, 2 * j), (1, 2 * j + 1)):
                inst = tn.matmul(ps_o[0:65, :],
                                 VT[p % 2][:, kt * 130 + h * 65:kt * 130 + (h + 1) * 65],
                                 PT[par][:, half * 512:(half + 1) * 512],
                                 start=(j == 0 and half == 0), stop=(j == npr - 1 and half == 1))
                if half == 1:
                    cx.inc(inst, "_pv", 1)

        def att_unit(tn, p, U):
            h, qc = U // 8, U % 8
            hr = h * 64
            P = 16 * p + U
            npr = npr_of(U)
            g0 = pb[P]
            for j in range(npr):
                g = g0 + j
                par = g % 2
                if g >= 2:
                    W(tn, "_ex", g - 1)
                tn.matmul(ps_s2[par].ap()[:, 0:512],
                          KT[p % 2][hr:hr + 64, (2 * j) * 128:(2 * j + 1) * 128],
                          QT[p % 2][hr:hr + 64, qc * 512:(qc + 1) * 512],
                          start=True, stop=True)
                cx.inc(tn.matmul(ps_s2[par].ap()[:, 512:1024],
                                 KT[p % 2][hr:hr + 64, (2 * j + 1) * 128:(2 * j + 2) * 128],
                                 QT[p % 2][hr:hr + 64, qc * 512:(qc + 1) * 512],
                                 start=True, stop=True), "_sp", 1)
                if j >= 2:
                    pv_pair(tn, p, P, h, g0, j - 2, npr)
            pv_pair(tn, p, P, h, g0, npr - 2, npr)
            pv_pair(tn, p, P, h, g0, npr - 1, npr)

        def _(tn):
            W(tn, "_xTe", 8 * NT)
            for p in range(9):
                if p < 8:
                    W(tn, f"_wl{p % 2}", 96 * (p // 2 + 1))
                for u in range(NT):
                    if p < 8:
                        gen_tile(tn, p, u)
                    if p >= 1 and u % 2 == 1:
                        att_unit(tn, p - 1, (u - 1) // 2)
                if p < 8:
                    gen_tail(tn, p)
        block.tensor(_)

        def _(s):
            mkc = [0]

            def qk_evac(p, g):
                ig = g % 32
                for u, dst, srcq in ((0, QT, qtr0), (1, KT, qtrk)):
                    kq = 2 * g + u
                    W(s, "_qTp", kq + 1)
                    s.copy(out=dst[p % 2][:, ig * 128:(ig + 1) * 128], in_=srcq)
                    s.drain()
                    cx.inc(s.nop(), "_qTe", 1)

            def exp_unit(p, U):
                P = 16 * p + U
                npr = npr_of(U)
                qc = U % 8
                g0 = pb[P]
                for j in range(npr):
                    g = g0 + j
                    par = g % 2
                    if g >= 2:
                        W(s, "_pv", g - 1)
                    if j >= 2 * qc:
                        mkc[0] += 2
                        W(s, "_mk", mkc[0])
                    else:
                        W(s, "_sp", g + 1)
                    s.activation(out=PT[par][:], in_=ps_s2[par].ap()[:], func=AF.Exp, scale=0.125, bias=csb["minus20"][:])
                    s.drain()
                    cx.inc(s.nop(), "_ex", 1)
                W(s, "_oevv", P + 1)
                s.activation(out=lnt[:], in_=oT2[P % 2][64:65, :], func=AF.Ln)
                s.drain()
                if P >= 1:
                    W(s, "_rlw", 16 * P)
                s.activation(out=rl[:], in_=lnt[:], func=AF.Exp, scale=-1.0)
                s.drain()
                cx.inc(s.nop(), "_rc", 1)

            for p in range(9):
                for u in range(NT):
                    if p < 8:
                        gi = 32 * p + u
                        if u >= 1:
                            qk_evac(p, gi - 1)
                        W(s, "_vp", gi + 1)
                        if p == 0 and u == 0:
                            W(s, "_vtm", 1)
                        if p >= 2 and u == 0:
                            W(s, "_pv", pb[16 * (p - 1)])
                        s.copy(out=VT[p % 2][:].rearrange("p (i4 h c) -> p i4 h c", i4=NT, h=2)[:, u, :, 0:64],
                               in_=ps_v.rearrange("p (h c) -> p h c", h=2))
                        s.drain()
                        cx.inc(s.nop(), "_vev", 1)
                    if p >= 1 and u % 2 == 1:
                        exp_unit(p - 1, (u - 1) // 2)
                if p < 8:
                    qk_evac(p, 32 * p + NT - 1)
        block.scalar(_)

        def _(v):
            W(v, "_tb", TBV)

            def rope_tile(p, u):
                gi = 32 * p + u
                cosv = cos_sb[:, u * 32:(u + 1) * 32].unsqueeze(1).broadcast_to((128, 4, 32))
                sinv = sin_sb[:, u * 32:(u + 1) * 32].unsqueeze(1).broadcast_to((128, 4, 32))
                W(v, "_qkp", gi + 1)
                qk = ps_qk.ap()[:, 0:256]
                ev = qk.rearrange("p (g two c) -> p g two c", two=2, c=32)[:, :, 0, :]
                od = qk.rearrange("p (g two c) -> p g two c", two=2, c=32)[:, :, 1, :]
                rev = rot[:].rearrange("p (g two c) -> p g two c", two=2, c=32)[:, :, 0, :]
                rod = rot[:].rearrange("p (g two c) -> p g two c", two=2, c=32)[:, :, 1, :]
                tev = rtm[:].rearrange("p (g two c) -> p g two c", two=2, c=32)[:, :, 0, :]
                tod = rtm[:].rearrange("p (g two c) -> p g two c", two=2, c=32)[:, :, 1, :]
                v.tensor_tensor(out=rev, in0=ev, in1=cosv, op=ALU.mult)
                v.tensor_tensor(out=tev, in0=od, in1=sinv, op=ALU.mult)
                v.drain()
                v.tensor_tensor(out=rev, in0=rev, in1=tev, op=ALU.subtract)
                v.tensor_tensor(out=tod, in0=ev, in1=sinv, op=ALU.mult)
                v.drain()
                v.tensor_tensor(out=rod, in0=od, in1=cosv, op=ALU.mult)
                v.drain()
                cx.inc(v.nop(), "_ro", 1)
                v.tensor_tensor(out=rod, in0=rod, in1=tod, op=ALU.add)
                v.drain()
                if gi >= 1:
                    W(v, "_qTp", 2 * gi)
                v.tensor_copy(qkbf[:], rot[:])
                v.drain()
                cx.inc(v.nop(), "_roc", 1)

            omq = []

            def om_emit():
                h, qc, P = omq.pop(0)
                if P >= 16:
                    W(v, "_oTd", 16 * (P // 16))
                W(v, "_rlb", 16 * (P + 1))
                v.tensor_tensor(out=outT[h * 64:h * 64 + 64, qc * 512:(qc + 1) * 512],
                                in0=oT2[P % 2][0:64, :], in1=rlb[:], op=ALU.mult)
                v.drain()
                cx.inc(v.nop(), "_om", 1)

            def vec_unit(p, U):
                h, qc = U // 8, U % 8
                P = 16 * p + U
                npr = npr_of(U)
                g0 = pb[P]
                if omq:
                    om_emit()
                for j in (2 * qc, 2 * qc + 1):
                    g = g0 + j
                    par = g % 2
                    W(v, "_sp", g + 1)
                    for bb in range(2):
                        b = 2 * (j - 2 * qc) + bb
                        v.tensor_tensor(out=ps_s2[par].ap()[:, bb * 512:bb * 512 + (b + 1) * 128],
                                        in0=ps_s2[par].ap()[:, bb * 512:bb * 512 + (b + 1) * 128],
                                        in1=bmask[b][:, 0:(b + 1) * 128], op=ALU.add)
                        v.drain()
                        cx.inc(v.nop(), "_mk", 1)
                W(v, "_pv", pb[P + 1])
                v.tensor_copy(oT2[P % 2][:], ps_o[0:65, :])
                v.drain()
                cx.inc(v.nop(), "_oevv", 1)
                omq.append((h, qc, P))

            for p in range(9):
                for u in range(NT):
                    if p < 8:
                        rope_tile(p, u)
                    if p >= 1 and u % 2 == 1:
                        vec_unit(p - 1, (u - 1) // 2)
            while omq:
                om_emit()
        block.vector(_)

        def _(sync):
            for p in range(8):
                W(sync, "_om", 16 * (p + 1))
                cx.inc(sync.dma_start(out=outT_sp[p], in_=outT[:]), "_oTd", 16)
        block.sync(_)

        x2pp = [att.enter_context(nc.sbuf_tensor(t + f"_x2p{i}", [128, D], F32)) for i in range(2)]

        def _(sync):
            W(sync, "_oTd", 128)
            for c in range(8):
                cx.inc(sync.dma_start(out=xn1T[:, c * S:(c + 1) * S], in_=outT_sp[c]), "_orb", 16)
            for i in range(NT):
                if i >= 2:
                    W(sync, "_x2v", i - 1)
                cx.inc(sync.dma_start(out=xb[i % 2][:], in_=x_res[i * 128:(i + 1) * 128, :]), f"_l2x{i % 2}", 16)
                if i >= 2:
                    j = i - 2
                    cx.inc(sync.dma_start(out=out[j * 128:(j + 1) * 128, :], in_=x2pp[j % 2][:]), f"_x2wo{j % 2}", 16)
                    cx.inc(sync.dma_start(out=x_res[j * 128:(j + 1) * 128, :], in_=x2pp[j % 2][:]), f"_x2wr{j % 2}", 16)
            for j in (NT - 2, NT - 1):
                W(sync, "_x2v", j + 1)
                cx.inc(sync.dma_start(out=out[j * 128:(j + 1) * 128, :], in_=x2pp[j % 2][:]), f"_x2wo{j % 2}", 16)
                cx.inc(sync.dma_start(out=x_res[j * 128:(j + 1) * 128, :], in_=x2pp[j % 2][:]), f"_x2wr{j % 2}", 16)
        block.sync(_)

        def _(gp):
            W(gp, "_om", 128)
            cx.inc(gp.dma_start(out=wo_bf[:].rearrange("p (c n) -> p c n", c=8),
                                in_=wqkvo["wo"].rearrange("(c p) n -> p c n", p=128)), "_wol", 16)
            cx.inc(gp.dma_start(out=bo_sb[:], in_=bqkvo["bo"]), "_wol", 16)
        block.gpsimd(_)

        def _(tn):
            W(tn, "_orb", 128)
            W(tn, "_wol", 32)
            for i in range(NT):
                pp2 = ps_s2[i % 2]
                if i >= 2:
                    W(tn, "_x2v", i - 1)
                for half in range(2):
                    for c in range(8):
                        tn.matmul(pp2.ap()[:, half * 512:(half + 1) * 512], xn1T[:, c * S + i * 128:c * S + (i + 1) * 128],
                                  wo_bf[:, c * 1024 + half * 512:c * 1024 + (half + 1) * 512],
                                  start=(c == 0), stop=False)
                    cx.inc(tn.matmul(pp2.ap()[:, half * 512:(half + 1) * 512], csb["ones1x128"][:], bo_sb[:, half * 512:(half + 1) * 512], start=False, stop=True), "_ap", 1)
        block.tensor(_)

        def _(v):
            for i in range(NT):
                pp2 = ps_s2[i % 2]
                W(v, "_ap", 2 * (i + 1))
                W(v, f"_l2x{i % 2}", 16 * (i // 2 + 1))
                if i >= 2:
                    W(v, f"_x2wo{i % 2}", 16 * (i // 2))
                    W(v, f"_x2wr{i % 2}", 16 * (i // 2))
                v.tensor_tensor(out=x2pp[i % 2][:], in0=xb[i % 2][:], in1=pp2.ap()[:], op=ALU.add)
                v.drain()
                cx.inc(v.nop(), "_x2v", 1)
        block.vector(_)

        att.close()
        aps.close()
        return [("_x2wo0", cx.val("_x2wo0")), ("_x2wo1", cx.val("_x2wo1")),
                ("_x2wr0", cx.val("_x2wr0")), ("_x2wr1", cx.val("_x2wr1"))]

    if "ffn1" in parts:
        emit_ffn("a", x_in, x_res, True)
    if "attn" in parts:
        fence2 = emit_attn("c", finals[-1])
        finals.extend(fence2)
    if "ffn2" in parts:
        emit_ffn("b", x_res[0:S, :], x_res, False, fence=fence2)


    fce = ExitStack()
    cb = [fce.enter_context(nc.sbuf_tensor(f"fc_b{i}", [128, D], F32)) for i in range(2)]

    def _(sync):
        for name, val in finals:
            sync.wait_ge(cx.sem(name), val)
        for i in range(NT):
            cx.inc(sync.dma_start(out=out[i * 128:(i + 1) * 128, :], in_=x_res[i * 128:(i + 1) * 128, :]), "fc_st0", 16)
        sync.wait_ge(cx.sem("fc_st0"), 16 * NT)
    block.sync(_)
    fce.close()
    es.close()
    return nc


def make_in_maps(L0_attn_x, L0_attn_wq, L0_attn_bq, L0_attn_wk, L0_attn_bk,
                 L0_attn_wv, L0_attn_bv, L0_attn_wo, L0_attn_bo,
                 L0_ffn_gate_w, L0_ffn_w1, L0_ffn_w2):
    consts = host_consts()
    perm = np.concatenate([h * DH + np.r_[np.arange(0, DH, 2), np.arange(1, DH, 2)]
                           for h in range(H)])
    shared = {
        "wq": np.ascontiguousarray(L0_attn_wq[:, perm]), "wk": np.ascontiguousarray(L0_attn_wk[:, perm]),
        "wv": np.ascontiguousarray(L0_attn_wv), "wo": np.ascontiguousarray(L0_attn_wo),
        "bq": np.ascontiguousarray(L0_attn_bq[perm]).reshape(1, D),
        "bk": np.ascontiguousarray(L0_attn_bk[perm]).reshape(1, D),
        "bv": np.ascontiguousarray(L0_attn_bv).reshape(1, D),
        "bo": np.ascontiguousarray(L0_attn_bo).reshape(1, D),
        "gate_wT": np.ascontiguousarray(L0_ffn_gate_w.T),
        "w1": np.ascontiguousarray(L0_ffn_w1), "w2": np.ascontiguousarray(L0_ffn_w2),
    }
    for k, v in consts.items():
        shared["c_" + k] = v
    in_maps = []
    for b in range(L0_attn_x.shape[0]):
        m = dict(shared)
        m["x"] = np.ascontiguousarray(L0_attn_x[b])
        in_maps.append(m)
    return in_maps


def kernel(**inputs):
    nc = build_program()
    in_maps = make_in_maps(**inputs)
    res = run_bass_kernel_spmd(nc, in_maps, core_ids=list(range(NCORES)))
    return np.stack([res.results[b]["out"] for b in range(NCORES)], axis=0)


# version 30 (final)
# speedup vs baseline: 1.0169x; 1.0169x over previous
import numpy as np
from contextlib import ExitStack

import concourse.bass as bass
import concourse.mybir as mybir
from concourse.bass import IndirectOffsetOnAxis
from concourse.bass_utils import run_bass_kernel_spmd

F32 = mybir.dt.float32
BF16 = mybir.dt.bfloat16
U32 = mybir.dt.uint32
I32 = mybir.dt.int32
ALU = mybir.AluOpType
AF = mybir.ActivationFunctionType
AX = mybir.AxisListType.X

S, D, H, DH, E, FF = 4096, 1024, 16, 64, 8, 4096
NT = S // 128
NCORES = 8
BISECT = 28
PAD = 128
SP = S + PAD


def host_consts():
    c = {}
    c["ident"] = np.eye(128, dtype=np.float32)
    kk = np.arange(128)[:, None]
    qq = np.arange(512)[None, :]
    for b in range(4):
        c[f"bandmask{b}"] = np.where(kk + b * 128 > qq, -1e9, 0.0).astype(np.float32)
    k2 = np.arange(128)
    c["blocktri"] = ((k2[:, None] // 16 == k2[None, :] // 16)
                     & (k2[:, None] < k2[None, :])).astype(np.float32)
    c["e16"] = (np.arange(128)[None, :] // 16 == np.arange(8)[:, None]).astype(np.float32)
    c["e16t"] = c["e16"].T.copy()
    c["ones8"] = np.ones((8, 1), np.float32)
    c["ones1x8"] = np.ones((1, 8), np.float32)
    c["ones1x64"] = np.ones((1, 64), np.float32)
    c["ones1x128"] = np.ones((1, 128), np.float32)
    p = np.arange(128)[:, None]
    f = np.arange(256)[None, :]
    tokid = ((p % 16) * 256 + f).astype(np.float32)
    c["tokid_u32"] = tokid.astype(np.uint32)
    c["tok8k"] = (tokid + 8192.0).astype(np.float32)
    c["basem1"] = ((p // 16) * 1024 - 1).astype(np.float32)
    c["bound"] = ((p // 16) * 1024 + 1023.5).astype(np.float32)
    c["minus20"] = np.full((128, 1), -20.0, np.float32)
    c["eps"] = np.full((128, 1), 1e-5, np.float32)
    padinit = np.zeros((128, 16), np.uint32)
    padinit[:, 0::2] = (4096 + np.arange(128))[:, None].astype(np.uint32)
    padinit[:, 1::2] = np.float32(0.0).view(np.uint32)
    c["padinit"] = padinit
    inv_freq = (1.0 / 10000.0 ** (np.arange(0, DH, 2, dtype=np.float32) / DH)).astype(np.float32)
    pos = np.arange(S, dtype=np.float32)
    freqs = (pos[:, None] * inv_freq[None, :]).astype(np.float32)
    c["cos_t"] = np.cos(freqs).astype(np.float32)
    c["sin_t"] = np.sin(freqs).astype(np.float32)
    return c


CONST_SPECS = {
    "ident": ([128, 128], F32, BF16),
    "blocktri": ([128, 128], F32, BF16),
    "e16": ([8, 128], F32, BF16), "e16t": ([128, 8], F32, BF16),
    "ones8": ([8, 1], F32, F32), "ones1x8": ([1, 8], F32, F32),
    "ones1x64": ([1, 64], F32, F32), "ones1x128": ([1, 128], F32, BF16),
    "tokid_u32": ([128, 256], U32, U32), "tok8k": ([128, 256], F32, F32),
    "basem1": ([128, 1], F32, F32), "bound": ([128, 1], F32, F32),
    "minus20": ([128, 1], F32, F32), "eps": ([128, 1], F32, F32),
    "padinit": ([128, 16], U32, U32),
}


class Ctx:
    def __init__(self, nc, es, block):
        self.nc, self.es, self.block = nc, es, block
        self.h, self.n = {}, {}

    def sem(self, name):
        if name not in self.h:
            self.h[name] = self.es.enter_context(self.nc.semaphore(name))
            self.n[name] = 0
        return self.h[name]

    def inc(self, inst, name, k=1):
        inst.then_inc(self.sem(name), k)
        self.n[name] += k
        return self.n[name]

    def val(self, name):
        self.sem(name)
        return self.n[name]


def build_program(parts=("ffn1", "attn", "ffn2")):
    nc = bass.Bass("TRN2", target_bir_lowering=False, debug=False)

    x_in = nc.dram_tensor("x", [S, D], F32, kind="ExternalInput").ap()
    wqkvo = {m: nc.dram_tensor(m, [D, D], F32, kind="ExternalInput").ap()
             for m in ("wq", "wk", "wv", "wo")}
    bqkvo = {m: nc.dram_tensor(m, [1, D], F32, kind="ExternalInput").ap()
             for m in ("bq", "bk", "bv", "bo")}
    gate_wT = nc.dram_tensor("gate_wT", [D, E], F32, kind="ExternalInput").ap()
    w1 = nc.dram_tensor("w1", [E, D, FF], F32, kind="ExternalInput").ap()
    w2 = nc.dram_tensor("w2", [E, FF, D], F32, kind="ExternalInput").ap()
    cdram = {k: nc.dram_tensor("c_" + k, sh, dt, kind="ExternalInput").ap()
             for k, (sh, dt, _) in CONST_SPECS.items()}
    for b in range(4):
        cdram[f"bandmask{b}"] = nc.dram_tensor(f"c_bandmask{b}", [128, 512], F32, kind="ExternalInput").ap()
    cdram["cos_t"] = nc.dram_tensor("c_cos_t", [S, 32], F32, kind="ExternalInput").ap()
    cdram["sin_t"] = nc.dram_tensor("c_sin_t", [S, 32], F32, kind="ExternalInput").ap()

    out = nc.dram_tensor("out", [S, D], F32, kind="ExternalOutput").ap()

    x_res = nc.dram_tensor("x_res", [SP, D], F32, kind="Internal").ap()
    xn_sp = nc.dram_tensor("xn_sp", [SP, D], BF16, kind="Internal").ap()
    probsT_d = nc.dram_tensor("probsT_d", [E, S], F32, kind="Internal").ap()
    scoresT_d = nc.dram_tensor("scoresT_d", [E, S], F32, kind="Internal").ap()
    idx_d = nc.dram_tensor("idx_d", [128, 256], U32, kind="Internal").ap()
    pay_d = nc.dram_tensor("pay_d", [128, 512], U32, kind="Internal").ap()
    m_g_d = nc.dram_tensor("m_g_d", [E * 1024, 2], U32, kind="Internal").ap()
    outT_sp = nc.dram_tensor("outT_sp", [8, 128, S], BF16, kind="Internal").ap()
    rl_d = nc.dram_tensor("rl_d", [1, 512], F32, kind="Internal").ap()

    es = ExitStack()
    block = es.enter_context(nc.Block())
    cx = Ctx(nc, es, block)

    csb = {}
    for k, (sh, dt, sbdt) in CONST_SPECS.items():
        csb[k] = es.enter_context(nc.sbuf_tensor("s_" + k, sh, sbdt))

    def _(sync):
        for k, (sh, dt, sbdt) in CONST_SPECS.items():
            if sbdt == dt:
                cx.inc(sync.dma_start(out=csb[k][:], in_=cdram[k]), "c_ld", 16)
    block.sync(_)

    def _(gp):
        for k, (sh, dt, sbdt) in CONST_SPECS.items():
            if sbdt != dt:
                cx.inc(gp.dma_start(out=csb[k][:], in_=cdram[k]), "c_ldg", 16)
    block.gpsimd(_)
    C_LD, C_LDG = cx.val("c_ld"), cx.val("c_ldg")

    finals = []

    def emit_ffn(t, res_in, res_scatter, copy_to_res, fence=None):
        ffn = ExitStack()
        sb = lambda name, sh, dt: ffn.enter_context(nc.sbuf_tensor(t + name, sh, dt))
        base = dict(cx.n)
        sv = lambda name: cx.val(name) - base.get(name, 0)

        def W(eng, name, v):
            eng.wait_ge(cx.sem(name), base.get(name, 0) + v)

        def FW(eng):
            if fence is not None:
                for fn, fv in fence:
                    eng.wait_ge(cx.sem(fn), fv)
        idx_e = [sb(f"_idxe{i}", [128, 32], U32) for i in range(2)]
        pay_e = [sb(f"_paye{i}", [128, 64], U32) for i in range(2)]
        mg_e = [sb(f"_mge{i}", [128, 16], U32) for i in range(2)]
        xin_st = sb("_xin", [128, 8 * D], BF16)
        xinT = sb("_xinT", [128, 8 * 1024], BF16)
        w1sb = [sb(f"_w1sb{i}", [128, 8 * 2048], BF16) for i in range(2)]
        hsb = sb("_hsb", [128, 32 * 1024], BF16)
        ysb = [sb(f"_ysb{i}", [128, 512], F32) for i in range(7)]
        tk2 = [sb(f"_tk2{i}", [128, 16], U32) for i in range(2)]

        p01 = ExitStack()
        sbp = lambda name, sh, dt: p01.enter_context(nc.sbuf_tensor(t + name, sh, dt))
        xb = [sbp(f"_xb{i}", [128, D], F32) for i in range(2)]
        stat = sbp("_stat", [128, 8], F32)
        xnb = [sbp(f"_xn{i}", [128, D], BF16) for i in range(2)]
        xnT_st = [sbp(f"_xnT{i}", [128, 128], BF16) for i in range(16)]
        gateTc = sbp("_gateTc", [128, 64], BF16)
        p2pack = sbp("_p2pack", [40, 512], F32)
        pexp = p2pack.ap()[0:8, :]
        sc_ch = p2pack.ap()[32:40, :]
        sc2 = sbp("_sc2", [8, 512], F32)
        probsT128 = sbp("_pT128", [128, 256], F32)
        lo, hi, mid = sbp("_lo", [128, 1], F32), sbp("_hi", [128, 1], F32), sbp("_mid", [128, 1], F32)
        geb = sbp("_geb", [128, 256], BF16)
        partial = sbp("_part", [128, 1], F32)
        partial_bf = sbp("_partbf", [128, 1], BF16)
        sel8bf = sbp("_sel8", [8, 1], BF16)
        selsb = sbp("_selsb", [128, 1], F32)

        idxu = sbp("_idxu", [128, 256], U32)
        pay = sbp("_pay", [128, 512], U32)
        rowbf = sbp("_rowbf", [128, 1], BF16)


        rsc = ExitStack()
        tmp1 = rsc.enter_context(nc.sbuf_tensor(t + "_tmp1", [128, 256], F32))
        tmp2 = rsc.enter_context(nc.sbuf_tensor(t + "_tmp2", [128, 256], F32))
        posf = rsc.enter_context(nc.sbuf_tensor(t + "_posf", [128, 256], F32))
        overm = rsc.enter_context(nc.sbuf_tensor(t + "_overm", [128, 256], F32))

        p0ps = ExitStack()
        ps_tr = [p0ps.enter_context(nc.psum_tensor(t + f"_pstr{i}", [128, 128], BF16)) for i in range(2)]
        ps_sc = [p0ps.enter_context(nc.psum_tensor(t + f"_pssc{i}", [8, 128], F32)) for i in range(2)]

        if copy_to_res:
            def _(gp):
                FW(gp)
                gp.memset(xb[0][:], 0.0)
                gp.memset(xnb[0][:], 0.0)
                gp.drain()
                cx.inc(gp.dma_start(out=res_scatter[S:SP, :], in_=xb[0][:]), "_zp", 16)
                cx.inc(gp.dma_start(out=xn_sp[S:SP, :], in_=xnb[0][:]), "_zp", 16)
            block.gpsimd(_)
        ZPV = sv("_zp")

        def _(sync):
            FW(sync)
            if ZPV:
                W(sync, "_zp", ZPV)
            for i in range(NT):
                if i >= 2:
                    W(sync, "_xbfree", i - 1)
                cx.inc(sync.dma_start(out=xb[i % 2][:], in_=res_in[i * 128:(i + 1) * 128, :]), f"_ldx{i % 2}", 16)
                if i >= 7 and (i - 7) % 4 == 0:
                    c = (i - 7) // 4
                    W(sync, "_sce", 4 * (c + 1))
                    cx.inc(sync.dma_start(out=scoresT_d[:, c * 512:(c + 1) * 512], in_=sc_ch[:]), "_scd", 16)
            for c in range(7, 8):
                W(sync, "_sce", 4 * (c + 1))
                cx.inc(sync.dma_start(out=scoresT_d[:, c * 512:(c + 1) * 512], in_=sc_ch[:]), "_scd", 16)
        block.sync(_)

        def emit_evacs(s, m):
            for dc in range(8):
                W(s, "_xTp", 8 * m + dc + 1)
                s.copy(out=xnT_st[(m % 2) * 8 + dc][:], in_=ps_tr[dc % 2][:])
                s.drain()
                cx.inc(s.nop(), "_xTe", 1)

        def _(s):
            FW(s)
            s.wait_ge(cx.sem("c_ld"), C_LD)
            if ZPV:
                W(s, "_zp", ZPV)
            for i in range(NT):
                if i >= 2:
                    emit_evacs(s, i - 2)
                W(s, f"_ldx{i % 2}", 16 * (i // 2 + 1))
                if i >= 2:
                    W(s, f"_xnsp{i % 2}", 16 * (i // 2))
                    W(s, "_xTp", 8 * (i - 1))
                s.activation(out=xnb[i % 2][:], in_=xb[i % 2][:], func=AF.Square, accum_out=stat[:, 6:7])
                s.drain()
                cx.inc(s.nop(), "_sq", 1)
                W(s, "_var", i + 1)
                s.activation(out=stat[:, 3:4], in_=stat[:, 2:3], func=AF.Sqrt, bias=csb["eps"][:])
                s.drain()
                cx.inc(s.nop(), "_sd", 1)
            emit_evacs(s, NT - 2)
            emit_evacs(s, NT - 1)
        block.scalar(_)

        def _(v):
            FW(v)
            def sc_evac(m):
                if m < 0 or m >= NT:
                    return
                W(v, "_scp", m + 1)
                if m % 4 == 0 and m >= 4:
                    W(v, "_scd", 16 * (m // 4))
                v.tensor_copy(sc_ch[:, (m % 4) * 128:(m % 4 + 1) * 128], ps_sc[m % 2][:])
                v.drain()
                cx.inc(v.nop(), "_sce", 1)
            v.wait_ge(cx.sem("c_ld"), C_LD)
            for i in range(NT):
                sc_evac(i - 3)
                W(v, f"_ldx{i % 2}", 16 * (i // 2 + 1))
                v.tensor_reduce(out=stat[:, 0:1], in_=xb[i % 2][:], axis=AX, op=ALU.add)
                v.drain()
                v.tensor_scalar(out=stat[:, 0:1], in0=stat[:, 0:1], scalar1=1.0 / D, scalar2=None, op0=ALU.mult)
                W(v, "_sq", i + 1)
                v.tensor_scalar(out=stat[:, 1:2], in0=stat[:, 6:7], scalar1=1.0 / D, scalar2=None, op0=ALU.mult)
                v.drain()
                v.tensor_tensor(out=stat[:, 4:5], in0=stat[:, 0:1], in1=stat[:, 0:1], op=ALU.mult)
                v.drain()
                v.tensor_tensor(out=stat[:, 2:3], in0=stat[:, 1:2], in1=stat[:, 4:5], op=ALU.subtract)
                v.drain()
                cx.inc(v.nop(), "_var", 1)
                W(v, "_sd", i + 1)
                v.reciprocal(out=stat[:, 5:6], in_=stat[:, 3:4])
                v.drain()
                v.tensor_scalar(out=xnb[i % 2][:], in0=xb[i % 2][:], scalar1=stat[:, 0:1], scalar2=stat[:, 5:6], op0=ALU.subtract, op1=ALU.mult)
                v.drain()
                cx.inc(v.nop(), "_xn", 1)
            sc_evac(NT - 3)
            sc_evac(NT - 2)
            sc_evac(NT - 1)
        block.vector(_)

        def _(gp):
            FW(gp)
            cx.inc(gp.dma_start(out=gateTc[:].rearrange("p (c e) -> p c e", c=8), in_=gate_wT.rearrange("(c p) e -> p c e", p=128)), "_gTe", 16)
            for i in range(NT):
                W(gp, "_xn", i + 1)
                cx.inc(gp.dma_start(out=xn_sp[i * 128:(i + 1) * 128, :], in_=xnb[i % 2][:]), f"_xnsp{i % 2}", 16)
                if copy_to_res:
                    cx.inc(gp.dma_start(out=res_scatter[i * 128:(i + 1) * 128, :], in_=xb[i % 2][:]), f"_rescp{i % 2}", 16)
                    W(gp, f"_rescp{i % 2}", 16 * (i // 2 + 1))
                cx.inc(gp.nop(), "_xbfree", 1)
        block.gpsimd(_)

        def _(tn):
            FW(tn)
            tn.wait_ge(cx.sem("c_ldg"), C_LDG)
            W(tn, "_gTe", 16)
            for i in range(NT):
                W(tn, "_xn", i + 1)
                base = 16 * i
                for dc in range(8):
                    k = base + dc
                    if k >= 2:
                        W(tn, "_xTe", sv("_xTp") - 1)
                    cx.inc(tn.transpose(ps_tr[dc % 2][:], xnb[i % 2][:, dc * 128:(dc + 1) * 128], csb["ident"][:]), "_xTp", 1)
                W(tn, "_xTe", sv("_xTp"))
                if i >= 1:
                    W(tn, "_sce", i)
                for dc in range(8):
                    inst = tn.matmul(ps_sc[i % 2][:], gateTc[:, dc * 8:(dc + 1) * 8], xnT_st[(i % 2) * 8 + dc][:],
                                     start=(dc == 0), stop=(dc == 7))
                cx.inc(inst, "_scp", 1)
        block.tensor(_)





        ps_s1 = p0ps.enter_context(nc.psum_tensor(t + "_pss1", [1, 512], F32))
        ps_pb = p0ps.enter_context(nc.psum_tensor(t + "_pspb", [8, 512], F32))
        ps_small = p0ps.enter_context(nc.psum_tensor(t + "_pssm", [128, 512], F32))
        ps_c = ps_small.ap()[0:8, 0:1]
        ps_c2 = ps_small.ap()[:, 1:2]
        ps_co = ps_small.ap()[:, 2:3]

        def _(sync):
            FW(sync)
            for c in range(8):
                if c >= 1:
                    W(sync, "_probc", c)
                    cx.inc(sync.dma_start(out=probsT_d[:, (c - 1) * 512:c * 512], in_=pexp[:]), "_pmul", 16)
                W(sync, "_scd", 16 * (c + 1))
                if c >= 1:
                    W(sync, "_pbc", c)
                cx.inc(sync.dma_start(out=sc2[:], in_=scoresT_d[:, c * 512:(c + 1) * 512]), "_sc2", 16)
            W(sync, "_probc", 8)
            cx.inc(sync.dma_start(out=probsT_d[:, 7 * 512:8 * 512], in_=pexp[:]), "_pmul", 16)
            W(sync, "_pmul", sv("_pmul"))
            cx.inc(sync.dma_start(out=probsT128[:], in_=probsT_d.rearrange("e (j f) -> (e j) f", f=256)), "_p128", 16)
        block.sync(_)

        def _(s):
            FW(s)
            for c in range(8):
                W(s, "_sc2", 16 * (c + 1))
                if c >= 1:
                    W(s, "_pmul", 16 * c)
                s.activation(out=pexp[:], in_=sc2[:], func=AF.Exp)
                s.drain()
                cx.inc(s.nop(), "_exp", 1)
        block.scalar(_)

        def _(tn):
            FW(tn)
            for c in range(8):
                W(tn, "_exp", c + 1)
                cx.inc(tn.matmul(ps_s1[:], csb["ones8"][:], pexp[:], start=True, stop=True), "_sum", 1)
                W(tn, "_rcpv", c + 1)
                cx.inc(tn.matmul(ps_pb[:], csb["ones1x8"][:], sc2[0:1, :], start=True, stop=True), "_pbc", 1)
        block.tensor(_)

        def _(v):
            FW(v)
            for c in range(8):
                W(v, "_sum", c + 1)
                v.reciprocal(out=sc2[0:1, :], in_=ps_s1[:])
                v.drain()
                cx.inc(v.nop(), "_rcpv", 1)
                W(v, "_pbc", c + 1)
                v.tensor_tensor(out=pexp[:], in0=pexp[:], in1=ps_pb[:], op=ALU.mult)
                v.drain()
                cx.inc(v.nop(), "_probc", 1)
        block.vector(_)



        def _(gp):
            FW(gp)
            gp.memset(lo[:], 0.0)
            gp.memset(hi[:], 1.0)
            gp.drain()
            cx.inc(gp.nop(), "_bi0", 1)
        block.gpsimd(_)

        def _(v):
            FW(v)
            W(v, "_p128", sv("_p128"))
            W(v, "_bi0", 1)
            for it in range(BISECT):
                v.tensor_scalar(out=mid[:], in0=lo[:], scalar1=hi[:], scalar2=0.5, op0=ALU.add, op1=ALU.mult)
                v.drain()
                v.tensor_scalar(out=geb[:], in0=probsT128[:], scalar1=mid[:], scalar2=None, op0=ALU.is_ge)
                v.drain()
                v.tensor_reduce(out=partial[:], in_=geb[:], axis=AX, op=ALU.add)
                v.drain()
                v.tensor_copy(partial_bf[:], partial[:])
                v.drain()
                cx.inc(v.nop(), "_biv", 1)
                W(v, "_bic", it + 1)
                v.tensor_scalar(out=sel8bf[:], in0=ps_c, scalar1=1024.0, scalar2=None, op0=ALU.is_ge)
                v.drain()
                cx.inc(v.nop(), "_bis", 1)
                W(v, "_bib", it + 1)
                v.tensor_copy(selsb[:], ps_c2)
                v.drain()
                v.tensor_tensor(out=tmp1[:, 0:1], in0=mid[:], in1=lo[:], op=ALU.subtract)
                v.tensor_tensor(out=tmp1[:, 1:2], in0=hi[:], in1=mid[:], op=ALU.subtract)
                v.drain()
                v.tensor_scalar(out=tmp1[:, 0:2], in0=tmp1[:, 0:2], scalar1=selsb[:], scalar2=None, op0=ALU.mult)
                v.drain()
                v.tensor_tensor(out=lo[:], in0=lo[:], in1=tmp1[:, 0:1], op=ALU.add)
                v.tensor_tensor(out=hi[:], in0=mid[:], in1=tmp1[:, 1:2], op=ALU.add)
                v.drain()
            v.tensor_scalar(out=geb[:], in0=probsT128[:], scalar1=lo[:], scalar2=None, op0=ALU.is_ge)
            v.drain()
            v.tensor_copy(tmp1[:], geb[:])
            v.drain()
            cur, nxt = tmp1, tmp2
            for sft in [1, 2, 4, 8, 16, 32, 64, 128]:
                v.tensor_copy(nxt[:, 0:sft], cur[:, 0:sft])
                v.tensor_tensor(out=nxt[:, sft:256], in0=cur[:, sft:256], in1=cur[:, 0:256 - sft], op=ALU.add)
                v.drain()
                cur, nxt = nxt, cur
            v.tensor_copy(rowbf[:], cur[:, 255:256])
            v.drain()
            cx.inc(v.nop(), "_rowc", 1)
            W(v, "_coar", 1)
            v.tensor_scalar(out=posf[:], in0=cur[:], scalar1=ps_co, scalar2=csb["basem1"][:], op0=ALU.add, op1=ALU.add)
            v.drain()
            v.select(out=tmp2[:], mask=geb.ap().bitcast(mybir.dt.int16), on_true=posf[:], on_false=csb["tok8k"][:], add_drain=True)
            v.drain()
            v.tensor_scalar(out=overm[:], in0=tmp2[:], scalar1=csb["bound"][:], scalar2=None, op0=ALU.is_gt)
            v.drain()
            v.select(out=posf[:], mask=overm.ap().bitcast(I32), on_true=csb["tok8k"][:], on_false=tmp2[:], add_drain=True)
            v.drain()
            v.tensor_copy(idxu[:], posf[:])
            v.drain()
            v.tensor_copy(pay[:, 0:512:2], csb["tokid_u32"][:])
            v.drain()
            v.tensor_scalar(out=overm[:], in0=probsT128[:], scalar1=0.5, scalar2=None, op0=ALU.mult)
            v.drain()
            v.tensor_copy(pay.ap().bitcast(F32)[:, 1:512:2], overm[:])
            v.drain()
            cx.inc(v.nop(), "_route", 1)
        block.vector(_)

        def _(tn):
            FW(tn)
            for it in range(BISECT):
                W(tn, "_biv", it + 1)
                cx.inc(tn.matmul(ps_c, csb["e16t"][:], partial_bf[:], start=True, stop=True), "_bic", 1)
                W(tn, "_bis", it + 1)
                cx.inc(tn.matmul(ps_c2, csb["e16"][:], sel8bf[:], start=True, stop=True), "_bib", 1)
            W(tn, "_rowc", 1)
            cx.inc(tn.matmul(ps_co, csb["blocktri"][:], rowbf[:], start=True, stop=True), "_coar", 1)
        block.tensor(_)

        def _(sync):
            FW(sync)
            W(sync, "_route", 1)
            cx.inc(sync.dma_start(out=idx_d, in_=idxu[:]), "_ipd", 16)
            cx.inc(sync.dma_start(out=pay_d, in_=pay[:]), "_ipd", 16)
            W(sync, "_ipd", sv("_ipd"))
            for e in range(E):
                if e >= 2:
                    W(sync, "_invdone", e - 1)
                src_i = idx_d[16 * e:16 * e + 16, :].rearrange("r (c8 k) -> (r c8) k", c8=8)
                cx.inc(sync.dma_start(out=idx_e[e % 2][:], in_=src_i), f"_ier{e % 2}", 16)
                src_p = pay_d[16 * e:16 * e + 16, :].rearrange("r (c8 k) -> (r c8) k", c8=8)
                cx.inc(sync.dma_start(out=pay_e[e % 2][:], in_=src_p), f"_ier{e % 2}", 16)
                W(sync, "_invdone", e + 1)
                if e >= 2:
                    W(sync, "_mgfree", e - 1)
                src = m_g_d[e * 1024:(e + 1) * 1024, :].rearrange("(p j) c -> p (j c)", p=128)
                cx.inc(sync.dma_start(out=mg_e[e % 2][:], in_=src), f"_mgr{e % 2}", 16)
        block.sync(_)

        rsc.close()
        p01.close()
        w2sb = [ffn.enter_context(nc.sbuf_tensor(t + f"_w2sb{i}", [128, 512], BF16)) for i in range(7)]
        p0ps.close()
        p2ps = ExitStack()
        pool = [p2ps.enter_context(nc.psum_tensor(t + f"_pool{i}", [128, 1024], F32)) for i in range(4)]
        trps = [pool[2].ap().bitcast(BF16)[:, 0:128], pool[3].ap().bitcast(BF16)[:, 0:128]]

        def _(gp):
            FW(gp)
            breg = gp.to_reg(8191)
            W(gp, "_route", 1)
            for e in range(E):
                cx.inc(gp.dma_start(out=m_g_d[e * 1024:(e + 1) * 1024, :].rearrange("(p j) c -> p (j c)", p=128),
                                    in_=csb["padinit"][:]), "_mgi", 16)
            W(gp, "_mgi", sv("_mgi"))
            W(gp, "_xnsp0", 16 * NT // 2)
            W(gp, "_xnsp1", 16 * NT // 2)
            def inv_gath_w1A_thunks(e):
                th = [lambda: W(gp, f"_ier{e % 2}", 32 * (e // 2 + 1))]

                def mk_scat(jj):
                    def f():
                        cx.inc(gp.indirect_dma_start(
                            out=m_g_d,
                            out_offset=IndirectOffsetOnAxis(ap=idx_e[e % 2][:, jj:jj + 1], axis=0),
                            in_=pay_e[e % 2][:, 2 * jj:2 * jj + 2],
                            in_offset=None, bounds_check=breg, oob_is_err=False), "_inv", 16)
                    return f
                th += [mk_scat(jj) for jj in range(32)]

                def f_done():
                    W(gp, "_inv", 16 * 32 * (e + 1))
                    cx.inc(gp.nop(), "_invdone", 1)
                    W(gp, f"_mgr{e % 2}", 16 * (e // 2 + 1))
                    if e >= 1:
                        W(gp, "_xinfree", e)
                th.append(f_done)
                mgx = mg_e[e % 2][:].rearrange("p (j c) -> p j c", c=2)

                def mk_gath(j):
                    def f():
                        cx.inc(gp.indirect_dma_start(
                            out=xin_st[:, j * D:(j + 1) * D],
                            out_offset=None, in_=xn_sp,
                            in_offset=IndirectOffsetOnAxis(ap=mgx[:, j:j + 1, 0:1], axis=0)), "_gat", 16)
                    return f
                th += [mk_gath(j) for j in range(8)]

                def f_w1w():
                    if e >= 1:
                        W(gp, "_hp", 32 * (e - 1) + 16)
                th.append(f_w1w)

                def mk_w1(dc):
                    def f():
                        cx.inc(gp.dma_start(out=w1sb[0][:, dc * 2048:(dc + 1) * 2048], in_=w1[e, dc * 128:(dc + 1) * 128, 0:2048]), "_w1lA", 16)
                    return f
                th += [mk_w1(dc) for dc in range(8)]
                return th

            th0 = inv_gath_w1A_thunks(0)
            for f in th0[-8:]:
                f()
            for f in th0[:-8]:
                f()
            for e in range(E):
                mg = mg_e[e % 2][:].rearrange("p (j c) -> p j c", c=2)

                def w2load(gfc):
                    half, fc = gfc // 32, gfc % 32
                    gn = 64 * e + gfc
                    nload = sum(sv(f"_w2l{r}") // 16 for r in range(7))
                    if nload == 0:
                        W(gp, "_route", 1)
                    if nload >= 7:
                        W(gp, "_w2free", nload - 6)
                    cx.inc(gp.dma_start(out=w2sb[gn % 7][:],
                                        in_=w2[e, fc * 128:(fc + 1) * 128, half * 512:(half + 1) * 512]), f"_w2l{gn % 7}", 16)

                H1POS = {4: 0, 5: 1, 6: 2, 7: 3, 0: 4, 1: 5, 2: 6, 3: 7}

                def scat(half, j):
                    pos = j if half == 0 else H1POS[j]
                    gj8 = 16 * e + 8 * half + pos
                    W(gp, "_yev", gj8 + 1)
                    if half == 0 and j == 0:
                        W(gp, "_tk2", e + 1)
                    W(gp, "_sca", sv("_sca"))
                    cx.inc(gp.indirect_dma_start(
                        out=res_scatter.rearrange("s (h c) -> (s h) c", h=2),
                        out_offset=IndirectOffsetOnAxis(ap=tk2[e % 2][:, 2 * j + half:2 * j + half + 1], axis=0),
                        in_=ysb[gj8 % 7][:], in_offset=None, compute_op=ALU.add), "_sca", 16)

                if e >= 1:
                    W(gp, "_hp", 32 * e)
                for dc in range(8):
                    cx.inc(gp.dma_start(out=w1sb[1][:, dc * 2048:(dc + 1) * 2048], in_=w1[e, dc * 128:(dc + 1) * 128, 2048:4096]), "_w1lB", 16)
                for gfc in range(7):
                    w2load(gfc)
                for f in (inv_gath_w1A_thunks(e + 1) if e + 1 < E else []):
                    f()
                for gfc in range(7, 64):
                    if gfc == 36:
                        scat(0, 0)
                    elif 40 <= gfc <= 60 and gfc % 4 == 0:
                        scat(0, (gfc - 36) // 4)
                    elif gfc == 62:
                        scat(0, 7)
                    w2load(gfc)
                for j in (4, 5, 6, 7, 0, 1, 2, 3):
                    scat(1, j)
                W(gp, "_sca", sv("_sca"))
                cx.inc(gp.nop(), "_mgfree", 1)
        block.gpsimd(_)

        def _(tn):
            FW(tn)
            for e in range(E):
                W(tn, "_gat", 16 * 8 * (e + 1))
                if e >= 1:
                    W(tn, "_yev", 16 * e - 4)
                for j in range(8):
                    for dc in range(8):
                        k = j * 8 + dc
                        if sv("_xiTp") >= 2:
                            W(tn, "_xiTe", sv("_xiTp") - 1)
                        cx.inc(tn.transpose(trps[k % 2], xin_st[:, j * D + dc * 128:j * D + (dc + 1) * 128], csb["ident"][:]), "_xiTp", 1)
                W(tn, "_xiTe", sv("_xiTp"))
                cx.inc(tn.nop(), "_xinfree", 1)
                W(tn, "_w1lA", 16 * 8 * (e + 1))
                for fh in range(2):
                    if fh == 1:
                        W(tn, "_w1lB", 16 * 8 * (e + 1))
                    for fcl in range(16):
                        fc = fh * 16 + fcl
                        if e >= 1 and fc == 0:
                            W(tn, "_yev", 16 * e - 2)
                        if e >= 1 and fc == 1:
                            W(tn, "_yev", 16 * e)
                        if fc >= 2:
                            W(tn, "_gel", 32 * e + fc - 1)
                        for dc in range(8):
                            lhs = w1sb[fh][:, dc * 2048 + fcl * 128:dc * 2048 + (fcl + 1) * 128]
                            tn.matmul(pool[fc % 2].ap()[:, 0:512], lhs, xinT[:, dc * 1024:dc * 1024 + 512], start=(dc == 0), stop=(dc == 7))
                            inst = tn.matmul(pool[fc % 2].ap()[:, 512:1024], lhs, xinT[:, dc * 1024 + 512:(dc + 1) * 1024], start=(dc == 0), stop=(dc == 7))
                        cx.inc(inst, "_hp", 1)
                W(tn, "_gel", 32 * (e + 1))
                for half in range(2):
                    for fc in range(32):
                        gn = 64 * e + half * 32 + fc
                        W(tn, f"_w2l{gn % 7}", 16 * (gn // 7 + 1))
                        for j in range(8):
                            if fc == 0 and half == 1:
                                W(tn, "_yev", 16 * e + j + 1)
                            lhs = hsb[:, fc * 1024 + j * 128:fc * 1024 + (j + 1) * 128]
                            inst = tn.matmul(pool[j // 2].ap()[:, (j % 2) * 512:(j % 2 + 1) * 512], lhs,
                                             w2sb[gn % 7][:], start=(fc == 0), stop=(fc == 31))
                        cx.inc(inst, "_w2free", 1)
        block.tensor(_)

        def _(s):
            FW(s)
            for e in range(E):
                for j in range(8):
                    for dc in range(8):
                        k = j * 8 + dc
                        W(s, "_xiTp", 64 * e + k + 1)
                        s.copy(out=xinT[:, dc * 1024 + j * 128:dc * 1024 + (j + 1) * 128], in_=trps[k % 2])
                        s.drain()
                        cx.inc(s.nop(), "_xiTe", 1)
                for fc in range(32):
                    W(s, "_hp", 32 * e + fc + 1)
                    s.activation(out=hsb[:, fc * 1024:(fc + 1) * 1024], in_=pool[fc % 2].ap()[:], func=AF.Gelu)
                    s.drain()
                    cx.inc(s.nop(), "_gel", 1)
        block.scalar(_)

        def _(v):
            FW(v)
            for e in range(E):
                mgf = mg_e[e % 2][:].rearrange("p (j c) -> p j c", c=2).bitcast(F32)
                if e >= 2:
                    W(v, "_sca", 256 * (e - 1))
                W(v, f"_mgr{e % 2}", 16 * (e // 2 + 1))
                mgtok = mg_e[e % 2][:].rearrange("p (j c) -> p j c", c=2)[:, :, 0:1].squeeze(2)
                tk2v = tk2[e % 2][:].rearrange("p (j two) -> p j two", two=2)
                v.tensor_scalar(out=tk2v[:, :, 0:1].squeeze(2), in0=mgtok, scalar1=1, scalar2=None, op0=ALU.logical_shift_left)
                v.tensor_scalar(out=tk2v[:, :, 1:2].squeeze(2), in0=mgtok, scalar1=1, scalar2=1, op0=ALU.logical_shift_left, op1=ALU.bitwise_or)
                v.drain()
                cx.inc(v.nop(), "_tk2", 1)
                for half in range(2):
                    W(v, "_w2free", 64 * e + 32 * (half + 1))
                    jorder = range(8) if half == 0 else (4, 5, 6, 7, 0, 1, 2, 3)
                    for pos, j in enumerate(jorder):
                        gj8 = 16 * e + 8 * half + pos
                        if gj8 >= 7:
                            W(v, "_sca", 16 * (gj8 - 6))
                        v.tensor_scalar(out=ysb[gj8 % 7][:], in0=pool[j // 2].ap()[:, (j % 2) * 512:(j % 2 + 1) * 512],
                                        scalar1=mgf[:, j:j + 1, 1:2].squeeze(1), scalar2=None, op0=ALU.mult)
                        v.drain()
                        cx.inc(v.nop(), "_yev", 1)
        block.vector(_)

        finals.append(("_sca", cx.val("_sca")))
        p2ps.close()
        ffn.close()


    def emit_attn(t, fence):
        att = ExitStack()
        sb = lambda name, sh, dt: att.enter_context(nc.sbuf_tensor(t + name, sh, dt))
        fs_name, fs_val = fence
        base = dict(cx.n)
        sv = lambda name: cx.val(name) - base.get(name, 0)

        def W(eng, name, v):
            eng.wait_ge(cx.sem(name), base.get(name, 0) + v)

        xb = [sb(f"_xb{i}", [128, D], F32) for i in range(2)]
        stat = sb("_stat", [128, 8], F32)
        xnb = [sb(f"_xn{i}", [128, D], BF16) for i in range(2)]
        xn1T = sb("_xn1T", [128, 8 * S], BF16)
        cos_sb = sb("_cos", [128, NT * 32], F32)
        sin_sb = sb("_sin", [128, NT * 32], F32)
        bmask = [sb(f"_bmk{i}", [128, 512], F32) for i in range(4)]
        wqk_sb = [sb(f"_wqk{i}", [128, 8 * 256], BF16) for i in range(2)]
        bqk_sb = [sb(f"_bqk{i}", [1, 256], BF16) for i in range(2)]
        wv_sb = [sb(f"_wv{i}", [128, 8 * 128], BF16) for i in range(2)]
        bv_sb = [sb(f"_bv{i}", [1, 128], BF16) for i in range(2)]
        QT = [sb(f"_QT{i}", [128, S], BF16) for i in range(2)]
        KT = [sb(f"_KT{i}", [128, S], BF16) for i in range(2)]
        VT = [sb(f"_VT{i}", [128, NT * 130], BF16) for i in range(2)]
        wo_bf = sb("_wo", [128, 8 * 1024], BF16)
        rot = sb("_rot", [128, 256], F32)
        rtm = sb("_rtm", [128, 256], F32)
        qkbf = sb("_qkbf", [128, 256], BF16)
        PT = [sb(f"_PT{i}", [128, 1024], BF16) for i in range(2)]
        oT2 = [sb(f"_oT{i}", [65, 512], F32) for i in range(2)]
        rl = sb("_rl", [1, 512], F32)
        lnt = sb("_lnt", [1, 512], F32)
        rlb = sb("_rlb", [64, 512], F32)
        outT = sb("_outT", [128, S], BF16)
        bo_sb = sb("_bo", [1, D], BF16)

        aps = ExitStack()
        ps_qk = aps.enter_context(nc.psum_tensor(t + "_psqk", [128, 512], F32))
        ps_vb = aps.enter_context(nc.psum_tensor(t + "_psvb", [128, 128], F32))
        ps_q = ps_qk.ap()[:, 0:128]
        ps_k = ps_qk.ap()[:, 128:256]
        ps_v = ps_vb.ap()[:, 0:128]
        ps_s2 = [aps.enter_context(nc.psum_tensor(t + f"_pss{i}", [128, 1024], F32)) for i in range(2)]
        ps_ob = aps.enter_context(nc.psum_tensor(t + "_pso", [128, 512], F32))
        ps_o = ps_ob.ap()
        ps_tr = aps.enter_context(nc.psum_tensor(t + "_pstr", [128, 512], BF16))
        qtr0 = ps_tr.ap()[:, 0:128]
        qtrk = ps_tr.ap()[:, 256:384]
        qtr = [qtr0, ps_s2[0].ap().bitcast(BF16)[:, 0:128]]

        def _(sync):
            sync.wait_ge(cx.sem(fs_name), fs_val)
            cx.inc(sync.dma_start(out=cos_sb[:].rearrange("p (i c) -> p i c", c=32), in_=cdram["cos_t"].rearrange("(i p) c -> p i c", p=128)), "_tb", 16)
            cx.inc(sync.dma_start(out=sin_sb[:].rearrange("p (i c) -> p i c", c=32), in_=cdram["sin_t"].rearrange("(i p) c -> p i c", p=128)), "_tb", 16)
            for b4 in range(4):
                cx.inc(sync.dma_start(out=bmask[b4][:], in_=cdram[f"bandmask{b4}"]), "_tb", 16)
            for i in range(NT):
                if i >= 2:
                    W(sync, "_xbfree", i - 1)
                cx.inc(sync.dma_start(out=xb[i % 2][:], in_=x_res[i * 128:(i + 1) * 128, :]), f"_ldx{i % 2}", 16)
        block.sync(_)
        TBV = sv("_tb")

        def _(s):
            s.wait_ge(cx.sem(fs_name), fs_val)
            s.wait_ge(cx.sem("c_ld"), C_LD)
            for i in range(NT):
                if i >= 1:
                    for dc in range(8):
                        k = 8 * (i - 1) + dc
                        W(s, "_xTp", k + 1)
                        s.copy(out=xn1T[:, dc * S + (i - 1) * 128:dc * S + i * 128], in_=qtr[k % 2])
                        s.drain()
                        cx.inc(s.nop(), "_xTe", 1)
                W(s, f"_ldx{i % 2}", 16 * (i // 2 + 1))
                if i >= 2:
                    W(s, "_xTp", 8 * (i - 1))
                s.activation(out=xnb[i % 2][:], in_=xb[i % 2][:], func=AF.Square, accum_out=stat[:, 6:7])
                s.drain()
                cx.inc(s.nop(), "_sq", 1)
                W(s, "_var", i + 1)
                s.activation(out=stat[:, 3:4], in_=stat[:, 2:3], func=AF.Sqrt, bias=csb["eps"][:])
                s.drain()
                cx.inc(s.nop(), "_sd", 1)
            for dc in range(8):
                k = 8 * (NT - 1) + dc
                W(s, "_xTp", k + 1)
                s.copy(out=xn1T[:, dc * S + (NT - 1) * 128:dc * S + NT * 128], in_=qtr[k % 2])
                s.drain()
                cx.inc(s.nop(), "_xTe", 1)
        block.scalar(_)

        def _(v):
            v.wait_ge(cx.sem(fs_name), fs_val)
            v.memset(VT[0][:], 1.0)
            v.memset(VT[1][:], 1.0)
            v.drain()
            cx.inc(v.nop(), "_vtm", 1)
            for i in range(NT):
                W(v, f"_ldx{i % 2}", 16 * (i // 2 + 1))
                v.tensor_reduce(out=stat[:, 0:1], in_=xb[i % 2][:], axis=AX, op=ALU.add)
                v.drain()
                v.tensor_scalar(out=stat[:, 0:1], in0=stat[:, 0:1], scalar1=1.0 / D, scalar2=None, op0=ALU.mult)
                W(v, "_sq", i + 1)
                v.tensor_scalar(out=stat[:, 1:2], in0=stat[:, 6:7], scalar1=1.0 / D, scalar2=None, op0=ALU.mult)
                v.drain()
                v.tensor_tensor(out=stat[:, 4:5], in0=stat[:, 0:1], in1=stat[:, 0:1], op=ALU.mult)
                v.drain()
                v.tensor_tensor(out=stat[:, 2:3], in0=stat[:, 1:2], in1=stat[:, 4:5], op=ALU.subtract)
                v.drain()
                cx.inc(v.nop(), "_var", 1)
                W(v, "_sd", i + 1)
                v.reciprocal(out=stat[:, 5:6], in_=stat[:, 3:4])
                v.drain()
                v.tensor_scalar(out=xnb[i % 2][:], in0=xb[i % 2][:], scalar1=stat[:, 0:1], scalar2=stat[:, 5:6], op0=ALU.subtract, op1=ALU.mult)
                v.drain()
                cx.inc(v.nop(), "_xn", 1)
        block.vector(_)

        def _(tn):
            tn.wait_ge(cx.sem(fs_name), fs_val)
            tn.wait_ge(cx.sem("c_ldg"), C_LDG)
            for i in range(NT):
                W(tn, "_xn", i + 1)
                for dc in range(8):
                    k = 8 * i + dc
                    if k >= 2:
                        W(tn, "_xTe", k - 1)
                    cx.inc(tn.transpose(qtr[k % 2], xnb[i % 2][:, dc * 128:(dc + 1) * 128], csb["ident"][:]), "_xTp", 1)
        block.tensor(_)

        def _(gp):
            gp.wait_ge(cx.sem(fs_name), fs_val)
            for i in range(NT):
                W(gp, "_xn", i + 1)
                cx.inc(gp.nop(), "_xbfree", 1)
            def rl_chain(P):
                W(gp, "_rc", P + 1)
                cx.inc(gp.dma_start(out=rl_d, in_=rl[:]), "_rlw", 16)
                W(gp, "_rlw", 16 * (P + 1))
                if P >= 1:
                    W(gp, "_om", P)
                cx.inc(gp.dma_start(out=rlb[:], in_=rl_d.broadcast_to((64, 512))), "_rlb", 16)

            for p in range(8):
                if p >= 2:
                    W(gp, "_vp", 32 * (p - 1))
                wqk2 = wqk_sb[p % 2][:].rearrange("p (c two n) -> p c two n", c=8, two=2)
                for two, m in ((0, "q"), (1, "k")):
                    cx.inc(gp.dma_start(
                        out=wqk2[:, :, two, :],
                        in_=wqkvo["w" + m].rearrange("(c p) n -> p c n", p=128)[:, :, 128 * p:128 * (p + 1)]), f"_wl{p % 2}", 16)
                    cx.inc(gp.dma_start(out=bqk_sb[p % 2][:, 128 * two:128 * (two + 1)], in_=bqkvo["b" + m][:, 128 * p:128 * (p + 1)]), f"_wl{p % 2}", 16)
                cx.inc(gp.dma_start(
                    out=wv_sb[p % 2][:].rearrange("p (c n) -> p c n", c=8),
                    in_=wqkvo["wv"].rearrange("(c p) n -> p c n", p=128)[:, :, 128 * p:128 * (p + 1)]), f"_wl{p % 2}", 16)
                cx.inc(gp.dma_start(out=bv_sb[p % 2][:], in_=bqkvo["bv"][:, 128 * p:128 * (p + 1)]), f"_wl{p % 2}", 16)
                if p >= 1:
                    for U in range(16):
                        rl_chain(16 * (p - 1) + U)
            for U in range(16):
                rl_chain(112 + U)
        block.gpsimd(_)

        NU = 16
        npr_of = lambda U: 2 * (U % 8) + 2
        pb = [0]
        for p in range(8):
            for U in range(NU):
                pb.append(pb[-1] + npr_of(U))

        def gen_tile(tn, p, u):
            gi = 32 * p + u
            if p == 0:
                W(tn, "_xTe", 8 * (u + 1))
            if gi >= 1:
                W(tn, "_ro", gi)
                W(tn, "_vev", gi)
            for dc in range(8):
                lhs = xn1T[:, dc * S + u * 128:dc * S + (u + 1) * 128]
                tn.matmul(ps_qk.ap()[:, 0:256], lhs, wqk_sb[p % 2][:, dc * 256:(dc + 1) * 256], start=(dc == 0), stop=False)
            cx.inc(tn.matmul(ps_qk.ap()[:, 0:256], csb["ones1x128"][:], bqk_sb[p % 2][:], start=False, stop=True), "_qkp", 1)
            if u >= 1:
                g = gi - 1
                W(tn, "_qTe", 2 * g)
                W(tn, "_roc", g + 1)
                cx.inc(tn.transpose(qtr0, qkbf[:, 0:128], csb["ident"][:]), "_qTp", 1)
            for dc in range(8):
                lhs = xn1T[:, dc * S + u * 128:dc * S + (u + 1) * 128]
                tn.matmul(ps_v, lhs, wv_sb[p % 2][:, dc * 128:(dc + 1) * 128], start=(dc == 0), stop=False)
            cx.inc(tn.matmul(ps_v, csb["ones1x128"][:], bv_sb[p % 2][:], start=False, stop=True), "_vp", 1)
            if u >= 1:
                g = gi - 1
                W(tn, "_qTe", 2 * g + 1)
                cx.inc(tn.transpose(qtrk, qkbf[:, 128:256], csb["ident"][:]), "_qTp", 1)

        def gen_tail(tn, p):
            g = 32 * p + NT - 1
            W(tn, "_qTe", 2 * g)
            W(tn, "_roc", g + 1)
            cx.inc(tn.transpose(qtr0, qkbf[:, 0:128], csb["ident"][:]), "_qTp", 1)
            W(tn, "_qTe", 2 * g + 1)
            cx.inc(tn.transpose(qtrk, qkbf[:, 128:256], csb["ident"][:]), "_qTp", 1)
            W(tn, "_qTe", 2 * g + 2)

        def pv_pair(tn, p, P, h, g0, j, npr):
            g = g0 + j
            par = g % 2
            W(tn, "_ex", g + 1)
            if j == 0 and P >= 1:
                W(tn, "_oevv", P)
            for half, kt in ((0, 2 * j), (1, 2 * j + 1)):
                inst = tn.matmul(ps_o[0:65, :],
                                 VT[p % 2][:, kt * 130 + h * 65:kt * 130 + (h + 1) * 65],
                                 PT[par][:, half * 512:(half + 1) * 512],
                                 start=(j == 0 and half == 0), stop=(j == npr - 1 and half == 1))
                if half == 1:
                    cx.inc(inst, "_pv", 1)

        def att_unit(tn, p, U):
            h, qc = U // 8, U % 8
            hr = h * 64
            P = 16 * p + U
            npr = npr_of(U)
            g0 = pb[P]
            for j in range(npr):
                g = g0 + j
                par = g % 2
                if g >= 2:
                    W(tn, "_ex", g - 1)
                tn.matmul(ps_s2[par].ap()[:, 0:512],
                          KT[p % 2][hr:hr + 64, (2 * j) * 128:(2 * j + 1) * 128],
                          QT[p % 2][hr:hr + 64, qc * 512:(qc + 1) * 512],
                          start=True, stop=True)
                cx.inc(tn.matmul(ps_s2[par].ap()[:, 512:1024],
                                 KT[p % 2][hr:hr + 64, (2 * j + 1) * 128:(2 * j + 2) * 128],
                                 QT[p % 2][hr:hr + 64, qc * 512:(qc + 1) * 512],
                                 start=True, stop=True), "_sp", 1)
                if j >= 2:
                    pv_pair(tn, p, P, h, g0, j - 2, npr)
            pv_pair(tn, p, P, h, g0, npr - 2, npr)
            pv_pair(tn, p, P, h, g0, npr - 1, npr)

        def _(tn):
            W(tn, "_xTe", 8 * NT)
            for p in range(9):
                if p < 8:
                    W(tn, f"_wl{p % 2}", 96 * (p // 2 + 1))
                for u in range(NT):
                    if p < 8:
                        gen_tile(tn, p, u)
                    if p >= 1 and u % 2 == 1:
                        att_unit(tn, p - 1, (u - 1) // 2)
                if p < 8:
                    gen_tail(tn, p)
        block.tensor(_)

        def _(s):
            mkc = [0]

            def qk_evac(p, g):
                ig = g % 32
                for u, dst, srcq in ((0, QT, qtr0), (1, KT, qtrk)):
                    kq = 2 * g + u
                    W(s, "_qTp", kq + 1)
                    s.copy(out=dst[p % 2][:, ig * 128:(ig + 1) * 128], in_=srcq)
                    s.drain()
                    cx.inc(s.nop(), "_qTe", 1)

            def exp_unit(p, U):
                P = 16 * p + U
                npr = npr_of(U)
                qc = U % 8
                g0 = pb[P]
                for j in range(npr):
                    g = g0 + j
                    par = g % 2
                    if g >= 2:
                        W(s, "_pv", g - 1)
                    if j >= 2 * qc:
                        mkc[0] += 2
                        W(s, "_mk", mkc[0])
                    else:
                        W(s, "_sp", g + 1)
                    s.activation(out=PT[par][:], in_=ps_s2[par].ap()[:], func=AF.Exp, scale=0.125, bias=csb["minus20"][:])
                    s.drain()
                    cx.inc(s.nop(), "_ex", 1)
                W(s, "_oevv", P + 1)
                s.activation(out=lnt[:], in_=oT2[P % 2][64:65, :], func=AF.Ln)
                s.drain()
                if P >= 1:
                    W(s, "_rlw", 16 * P)
                s.activation(out=rl[:], in_=lnt[:], func=AF.Exp, scale=-1.0)
                s.drain()
                cx.inc(s.nop(), "_rc", 1)

            for p in range(9):
                for u in range(NT):
                    if p < 8:
                        gi = 32 * p + u
                        if u >= 1:
                            qk_evac(p, gi - 1)
                        W(s, "_vp", gi + 1)
                        if p == 0 and u == 0:
                            W(s, "_vtm", 1)
                        if p >= 2 and u == 0:
                            W(s, "_pv", pb[16 * (p - 1)])
                        s.copy(out=VT[p % 2][:].rearrange("p (i4 h c) -> p i4 h c", i4=NT, h=2)[:, u, :, 0:64],
                               in_=ps_v.rearrange("p (h c) -> p h c", h=2))
                        s.drain()
                        cx.inc(s.nop(), "_vev", 1)
                    if p >= 1 and u % 2 == 1:
                        exp_unit(p - 1, (u - 1) // 2)
                if p < 8:
                    qk_evac(p, 32 * p + NT - 1)
            W(s, "_vp", 256)
            for c in range(7):
                W(s, "_oTd", 16 * (c + 1))
                cx.inc(s.dma_start(out=xn1T[:, c * S:(c + 1) * S], in_=outT_sp[c]), "_orb", 16)
        block.scalar(_)

        def _(v):
            W(v, "_tb", TBV)

            def rope_tile(p, u):
                gi = 32 * p + u
                cosv = cos_sb[:, u * 32:(u + 1) * 32].unsqueeze(1).broadcast_to((128, 4, 32))
                sinv = sin_sb[:, u * 32:(u + 1) * 32].unsqueeze(1).broadcast_to((128, 4, 32))
                W(v, "_qkp", gi + 1)
                qk = ps_qk.ap()[:, 0:256]
                ev = qk.rearrange("p (g two c) -> p g two c", two=2, c=32)[:, :, 0, :]
                od = qk.rearrange("p (g two c) -> p g two c", two=2, c=32)[:, :, 1, :]
                rev = rot[:].rearrange("p (g two c) -> p g two c", two=2, c=32)[:, :, 0, :]
                rod = rot[:].rearrange("p (g two c) -> p g two c", two=2, c=32)[:, :, 1, :]
                tev = rtm[:].rearrange("p (g two c) -> p g two c", two=2, c=32)[:, :, 0, :]
                tod = rtm[:].rearrange("p (g two c) -> p g two c", two=2, c=32)[:, :, 1, :]
                v.tensor_tensor(out=rev, in0=ev, in1=cosv, op=ALU.mult)
                v.tensor_tensor(out=tev, in0=od, in1=sinv, op=ALU.mult)
                v.drain()
                v.tensor_tensor(out=rev, in0=rev, in1=tev, op=ALU.subtract)
                v.tensor_tensor(out=tod, in0=ev, in1=sinv, op=ALU.mult)
                v.drain()
                v.tensor_tensor(out=rod, in0=od, in1=cosv, op=ALU.mult)
                v.drain()
                cx.inc(v.nop(), "_ro", 1)
                v.tensor_tensor(out=rod, in0=rod, in1=tod, op=ALU.add)
                v.drain()
                if gi >= 1:
                    W(v, "_qTp", 2 * gi)
                v.tensor_copy(qkbf[:], rot[:])
                v.drain()
                cx.inc(v.nop(), "_roc", 1)

            omq = []

            def om_emit():
                h, qc, P = omq.pop(0)
                if P >= 16:
                    W(v, "_oTd", 16 * (P // 16))
                W(v, "_rlb", 16 * (P + 1))
                v.tensor_tensor(out=outT[h * 64:h * 64 + 64, qc * 512:(qc + 1) * 512],
                                in0=oT2[P % 2][0:64, :], in1=rlb[:], op=ALU.mult)
                v.drain()
                cx.inc(v.nop(), "_om", 1)

            def vec_unit(p, U):
                h, qc = U // 8, U % 8
                P = 16 * p + U
                npr = npr_of(U)
                g0 = pb[P]
                if omq:
                    om_emit()
                for j in (2 * qc, 2 * qc + 1):
                    g = g0 + j
                    par = g % 2
                    W(v, "_sp", g + 1)
                    for bb in range(2):
                        b = 2 * (j - 2 * qc) + bb
                        v.tensor_tensor(out=ps_s2[par].ap()[:, bb * 512:bb * 512 + (b + 1) * 128],
                                        in0=ps_s2[par].ap()[:, bb * 512:bb * 512 + (b + 1) * 128],
                                        in1=bmask[b][:, 0:(b + 1) * 128], op=ALU.add)
                        v.drain()
                        cx.inc(v.nop(), "_mk", 1)
                W(v, "_pv", pb[P + 1])
                v.tensor_copy(oT2[P % 2][:], ps_o[0:65, :])
                v.drain()
                cx.inc(v.nop(), "_oevv", 1)
                omq.append((h, qc, P))

            for p in range(9):
                for u in range(NT):
                    if p < 8:
                        rope_tile(p, u)
                    if p >= 1 and u % 2 == 1:
                        vec_unit(p - 1, (u - 1) // 2)
            while omq:
                om_emit()
        block.vector(_)

        def _(sync):
            for p in range(8):
                W(sync, "_om", 16 * (p + 1))
                cx.inc(sync.dma_start(out=outT_sp[p], in_=outT[:]), "_oTd", 16)
        block.sync(_)

        x2pp = [att.enter_context(nc.sbuf_tensor(t + f"_x2p{i}", [128, D], F32)) for i in range(2)]

        def _(sync):
            W(sync, "_oTd", 128)
            cx.inc(sync.dma_start(out=xn1T[:, 7 * S:8 * S], in_=outT_sp[7]), "_orb", 16)
            for i in range(NT):
                if i >= 2:
                    W(sync, "_x2v", i - 1)
                cx.inc(sync.dma_start(out=xb[i % 2][:], in_=x_res[i * 128:(i + 1) * 128, :]), f"_l2x{i % 2}", 16)
                if i >= 2:
                    j = i - 2
                    cx.inc(sync.dma_start(out=out[j * 128:(j + 1) * 128, :], in_=x2pp[j % 2][:]), f"_x2wo{j % 2}", 16)
                    cx.inc(sync.dma_start(out=x_res[j * 128:(j + 1) * 128, :], in_=x2pp[j % 2][:]), f"_x2wr{j % 2}", 16)
            for j in (NT - 2, NT - 1):
                W(sync, "_x2v", j + 1)
                cx.inc(sync.dma_start(out=out[j * 128:(j + 1) * 128, :], in_=x2pp[j % 2][:]), f"_x2wo{j % 2}", 16)
                cx.inc(sync.dma_start(out=x_res[j * 128:(j + 1) * 128, :], in_=x2pp[j % 2][:]), f"_x2wr{j % 2}", 16)
        block.sync(_)

        def _(gp):
            W(gp, "_om", 128)
            cx.inc(gp.dma_start(out=wo_bf[:].rearrange("p (c n) -> p c n", c=8),
                                in_=wqkvo["wo"].rearrange("(c p) n -> p c n", p=128)), "_wol", 16)
            cx.inc(gp.dma_start(out=bo_sb[:], in_=bqkvo["bo"]), "_wol", 16)
        block.gpsimd(_)

        def _(tn):
            W(tn, "_orb", 128)
            W(tn, "_wol", 32)
            for i in range(NT):
                pp2 = ps_s2[i % 2]
                if i >= 2:
                    W(tn, "_x2v", i - 1)
                for half in range(2):
                    for c in range(8):
                        tn.matmul(pp2.ap()[:, half * 512:(half + 1) * 512], xn1T[:, c * S + i * 128:c * S + (i + 1) * 128],
                                  wo_bf[:, c * 1024 + half * 512:c * 1024 + (half + 1) * 512],
                                  start=(c == 0), stop=False)
                    cx.inc(tn.matmul(pp2.ap()[:, half * 512:(half + 1) * 512], csb["ones1x128"][:], bo_sb[:, half * 512:(half + 1) * 512], start=False, stop=True), "_ap", 1)
        block.tensor(_)

        def _(v):
            for i in range(NT):
                pp2 = ps_s2[i % 2]
                W(v, "_ap", 2 * (i + 1))
                W(v, f"_l2x{i % 2}", 16 * (i // 2 + 1))
                if i >= 2:
                    W(v, f"_x2wo{i % 2}", 16 * (i // 2))
                    W(v, f"_x2wr{i % 2}", 16 * (i // 2))
                v.tensor_tensor(out=x2pp[i % 2][:], in0=xb[i % 2][:], in1=pp2.ap()[:], op=ALU.add)
                v.drain()
                cx.inc(v.nop(), "_x2v", 1)
        block.vector(_)

        att.close()
        aps.close()
        return [("_x2wo0", cx.val("_x2wo0")), ("_x2wo1", cx.val("_x2wo1")),
                ("_x2wr0", cx.val("_x2wr0")), ("_x2wr1", cx.val("_x2wr1"))]

    if "ffn1" in parts:
        emit_ffn("a", x_in, x_res, True)
    if "attn" in parts:
        fence2 = emit_attn("c", finals[-1])
        finals.extend(fence2)
    if "ffn2" in parts:
        emit_ffn("b", x_res[0:S, :], x_res, False, fence=fence2)


    fce = ExitStack()
    cb = [fce.enter_context(nc.sbuf_tensor(f"fc_b{i}", [128, D], F32)) for i in range(2)]

    def _(sync):
        for name, val in finals:
            sync.wait_ge(cx.sem(name), val)
        for i in range(NT):
            cx.inc(sync.dma_start(out=out[i * 128:(i + 1) * 128, :], in_=x_res[i * 128:(i + 1) * 128, :]), "fc_st0", 16)
        sync.wait_ge(cx.sem("fc_st0"), 16 * NT)
    block.sync(_)
    fce.close()
    es.close()
    return nc


def make_in_maps(L0_attn_x, L0_attn_wq, L0_attn_bq, L0_attn_wk, L0_attn_bk,
                 L0_attn_wv, L0_attn_bv, L0_attn_wo, L0_attn_bo,
                 L0_ffn_gate_w, L0_ffn_w1, L0_ffn_w2):
    consts = host_consts()
    perm = np.concatenate([h * DH + np.r_[np.arange(0, DH, 2), np.arange(1, DH, 2)]
                           for h in range(H)])
    shared = {
        "wq": np.ascontiguousarray(L0_attn_wq[:, perm]), "wk": np.ascontiguousarray(L0_attn_wk[:, perm]),
        "wv": np.ascontiguousarray(L0_attn_wv), "wo": np.ascontiguousarray(L0_attn_wo),
        "bq": np.ascontiguousarray(L0_attn_bq[perm]).reshape(1, D),
        "bk": np.ascontiguousarray(L0_attn_bk[perm]).reshape(1, D),
        "bv": np.ascontiguousarray(L0_attn_bv).reshape(1, D),
        "bo": np.ascontiguousarray(L0_attn_bo).reshape(1, D),
        "gate_wT": np.ascontiguousarray(L0_ffn_gate_w.T),
        "w1": np.ascontiguousarray(L0_ffn_w1), "w2": np.ascontiguousarray(L0_ffn_w2),
    }
    for k, v in consts.items():
        shared["c_" + k] = v
    in_maps = []
    for b in range(L0_attn_x.shape[0]):
        m = dict(shared)
        m["x"] = np.ascontiguousarray(L0_attn_x[b])
        in_maps.append(m)
    return in_maps


def kernel(**inputs):
    nc = build_program()
    in_maps = make_in_maps(**inputs)
    res = run_bass_kernel_spmd(nc, in_maps, core_ids=list(range(NCORES)))
    return np.stack([res.results[b]["out"] for b in range(NCORES)], axis=0)
